# version 16
# baseline (speedup 1.0000x reference)
import contextlib
import math

import numpy as np
import ml_dtypes

import concourse.bass as bass
import concourse.mybir as mybir
from concourse.bass_utils import run_bass_kernel_spmd

F32 = mybir.dt.float32
BF16 = mybir.dt.bfloat16
I32 = mybir.dt.int32
AF = mybir.ActivationFunctionType
OP = mybir.AluOpType
AX = mybir.AxisListType

DM = 1024
SEQ = 8192
NB = 2
OWN = 2048
HALO = 1536
EXT = OWN + HALO
NFT = DM // 128
NH = 8
DH = 64
WIN = 512
HID2 = 5460
HID = 2730
DTOP = 128
QK_SCALE = 8.0
TOP_SCALE = 1.0 / math.sqrt(128.0)
EPS = 1e-5
CH = 512
NCH_E = EXT // CH
NCH_A = SEQ // CH
NG_E = EXT // 128
NG_A = SEQ // 128
BF = ml_dtypes.bfloat16


class B:

    def __init__(self, dbg=False, phases=99):
        self.nc = bass.Bass("TRN2", target_bir_lowering=False)
        self.st = contextlib.ExitStack()
        self._sems = {}
        self.semst = contextlib.ExitStack()
        self.cnt = {}
        self.epoch = 0
        self.dram = {}
        self.dbg = dbg
        self.phases = phases
        self.dbg_names = []

    def sem(self, name):
        if name not in self._sems:
            self._sems[name] = self.semst.enter_context(
                self.nc.semaphore(f"{name}_{self.epoch}"))
            self.cnt[name] = 0
        return self._sems[name]

    def phase_boundary(self):
        nc = self.nc
        self.epoch += 1
        clear = [(n, h) for n, h in self._sems.items()]
        alleng = [mybir.EngineType.PE, mybir.EngineType.DVE,
                  mybir.EngineType.Activation, mybir.EngineType.Pool,
                  mybir.EngineType.SP]
        with nc.Block() as blk:
            @blk.gpsimd
            def _(g):
                for n, h in clear:
                    if self.cnt[n] > 0:
                        g.wait_ge(h, self.cnt[n])
        nc.all_engine_barrier()
        with nc.Block() as blk:
            @blk.gpsimd
            def _(g):
                nums = sorted(h.num for _, h in clear)
                ranges = []
                for x in nums:
                    if ranges and ranges[-1][1] == x:
                        ranges[-1][1] = x + 1
                    else:
                        ranges.append([x, x + 1])
                for a, bnd in ranges:
                    g.dma_reset(range(a, bnd))
                for n, h in clear:
                    g.sem_clear(h)
        nc.all_engine_barrier()
        self.semst.close()
        self.semst = contextlib.ExitStack()
        self._sems = {}
        self.cnt = {}

    def inc(self, instr, name, amt=1):
        instr.then_inc(self.sem(name), amt)
        self.cnt[name] += amt
        return instr

    def w(self, eng, name, val=None):
        self.sem(name)
        eng.wait_ge(self._sems[name], self.cnt[name] if val is None else val)

    def sbuf(self, name, shape, dt=F32, st=None):
        return (st or self.st).enter_context(
            self.nc.sbuf_tensor("s_" + name, shape, dt))

    def psum(self, name, shape, dt=F32):
        return self.st.enter_context(self.nc.psum_tensor(name, shape, dt))

    def din(self, name, shape, dt=F32):
        self.dram[name] = self.nc.dram_tensor(name, list(shape), dt,
                                              kind="ExternalInput").ap()
        return self.dram[name]

    def dout(self, name, shape, dt=F32):
        self.dram[name] = self.nc.dram_tensor(name, list(shape), dt,
                                              kind="ExternalOutput").ap()
        return self.dram[name]

    def dint(self, name, shape, dt=F32):
        self.dram[name] = self.nc.dram_tensor(name, list(shape), dt,
                                              kind="Internal").ap()
        return self.dram[name]

    def declare(self):
        din = self.din
        din("xT", (DM, SEQ))
        din("xTe", (DM, EXT))
        din("wq", (DM, DTOP)); din("wk", (DM, DTOP)); din("wv", (DM, DTOP))
        din("wo_top", (DTOP, DM))
        din("mask_top", (DTOP, 1), I32)
        for l in range(2):
            din(f"wqkv{l}", (DM, 3 * 512))
            din(f"wo{l}", (512, DM))
            din(f"w1{l}", (DM, HID2))
            din(f"w2{l}", (HID, DM))
            din(f"b1{l}", (1, HID2))
            din(f"b2{l}", (1, DM))
            din(f"lng{l}", (1, DM))
            din(f"lnb{l}", (1, DM))
            din(f"qs{l}", (1, DH))
            din(f"ks{l}", (1, DH))
        din("csq", (128, EXT), BF16); din("ssq", (128, EXT), BF16)
        din("csk", (128, EXT), BF16); din("ssk", (128, EXT), BF16)
        din("identb", (128, 128), BF16)
        din("identf", (128, 128), F32)
        din("psw", (128, 128), BF16)
        din("trimasks", (128, 256), BF16)
        din("kvc", (128, 28))
        din("tvr", (NCH_E, CH))
        din("bheadb_c", (2, 128), BF16)
        self.dout("y", (DM, EXT))
        if self.dbg:
            self.dout("dbgM", (128, 4224 * 7), BF16)
            self.dout("dbgOH", (128, 2048 * 7), BF16)
        self.dint("st_x1", (DM, EXT))
        self.dint("st_in", (DM, EXT))
        self.dint("w1bf", (DM, HID2), BF16)
        self.dint("w2bf", (HID, DM), BF16)

        self.st1 = contextlib.ExitStack()
        s = self.sbuf
        s1 = lambda n, sh, dt=F32: self.sbuf(n, sh, dt, st=self.st1)
        self.identb = s("identb", [128, 128], BF16)
        self.identf = s("identf", [128, 128], F32)
        self.onesb = s("onesb", [128, 128], BF16)
        self.onesrow = s("onesrow", [1, 128])
        self.rcol_e = s("rcol_e", [128, NG_E])
        self.mucol = s("mucol", [128, 4])
        self.m2col = s("m2col", [128, 4])
        self.varcol = s("varcol", [128, 4])
        self.sdcol = s("sdcol", [128, 4])
        self.sxrow_e = s("sxrow_e", [NCH_E, CH])
        self.r_row_e = s("r_row_e", [1, CH])
        self.xs = [s(f"xs{i}", [128, NFT * CH]) for i in range(2)]
        self.xbf = s("xbf", [128, NFT * 528], BF16)
        self.x2bf = s("x2bf", [128, NFT * 528], BF16)
        self.srow = s("srow", [1, 2 * CH])
        self.srow_b = s("srow_b", [1, 2 * CH], BF16)
        self.vraw = s("vraw", [128, CH], BF16)
        self.plane = s("plane", [128, CH])
        self.rows64 = s("rows64", [64, 128])
        self.mask_i = s("mask_i", [DTOP, 1], I32)
        self.mask_f = s("mask_f", [DTOP, 1])
        self.epscol = s("epscol", [128, 1])
        self.r1row = s("r1row", [NCH_E, CH])
        self.nmr1row = s("nmr1row", [NCH_E, CH])
        self.tvrsb = s("tvrsb", [NCH_E, CH])
        self.yrows = s("yrows", [1, 6 * CH])
        self.rstg = s("rstg", [1, CH])
        self.nstg = s("nstg", [1, CH])
        self.sxstg = s("sxstg", [1, CH])
        self.kT_top = s1("kT_top", [128, SEQ], BF16)
        self.vTM_top = s1("vTM_top", [128, NG_A * 132], BF16)
        self.qT_top = s1("qT_top", [128, EXT], BF16)
        self.rcol_a = s1("rcol_a", [128, NG_A])
        self.rtcol_a = s1("rtcol_a", [128, NG_A])
        self.wkvq = s1("wkvq", [128, NFT * 3 * DTOP], BF16)
        self.csumr = s1("csumr", [1, 3 * DTOP])
        self.csumr_b = s1("csumr_b", [1, 3 * DTOP], BF16)
        self.oh_top = s1("oh_top", [128, EXT], BF16)
        self.pb = [s1(f"pb{i}", [128, CH], BF16) for i in range(4)]
        self.rdrowb = s1("rdrowb", [1, CH], BF16)
        self.osb = s1("osb", [128, CH])
        self.ybf = s1("ybf", [128, NFT * CH], BF16)
        self.wo_b = s1("wo_b", [128, DM], BF16)
        self.rwcol = s1("rwcol", [128, 1])
        self.rwcol_b = s1("rwcol_b", [128, 1], BF16)
        self.nmy = s1("nmy", [1, CH])
        self.nmy_b = s1("nmy_b", [1, CH], BF16)
        self.x1s = s1("x1s", [128, NFT * CH])
        self.utile = s1("utile", [128, CH])

        self.psA = self.psum("psA", [128, 2048])
        self.psB = self.psum("psB", [128, 2048])

    def phase_init(self):
        nc = self.nc
        with nc.Block() as blk:
            @blk.sync
            def _(sy):
                self.inc(sy.dma_start(out=self.identb[:, :], in_=self.dram["identb"]), "ld0", 16)
                self.inc(sy.dma_start(out=self.identf[:, :], in_=self.dram["identf"]), "ld0", 16)
                self.inc(sy.dma_start(out=self.mask_i[:, :], in_=self.dram["mask_top"]), "ld0", 16)

            @blk.vector
            def _(v):
                v.memset(self.onesb[:, :], 1.0)
                v.memset(self.onesrow[:, :], 1.0)
                v.memset(self.epscol[:, :], EPS)
                self.w(v, "ld0")
                self.inc(v.tensor_copy(self.mask_f[:, :], self.mask_i[:, :]), "initv", 1)

    def phase_top_kvq(self):
        nc = self.nc
        dram = self.dram
        psA, psB = self.psA, self.psB
        psBb = psB.ap().bitcast(BF16)

        with nc.Block() as blk:
            @blk.gpsimd
            def _(g):
                for j, wn in enumerate(("wq", "wk", "wv")):
                    self.inc(g.dma_start(
                        out=self.wkvq.ap().rearrange("p (a c) -> p a c", a=NFT)
                        [:, :, 128 * j:128 * (j + 1)],
                        in_=dram[wn].rearrange("(a p) c -> p a c", p=128)),
                        "wldg", 16)

            @blk.tensor
            def _(t):
                self.w(t, "wldg", 48)
                for ft in range(NFT):
                    ins = t.matmul(psA[0:1, 0:384], self.onesb[:, 0:1],
                                   self.wkvq[:, ft * 384:(ft + 1) * 384],
                                   start=(ft == 0), stop=(ft == NFT - 1))
                self.inc(ins, "csmm", 1)

            @blk.vector
            def _(v):
                self.w(v, "csmm", 1)
                self.inc(v.tensor_scalar(self.csumr[:, :], psA[0:1, 0:384],
                                         -1.0 / DM, None, OP.mult), "vch", 1)
                self.w(v, "vch")
                self.inc(v.tensor_copy(self.csumr_b[:, :], self.csumr[:, :]),
                         "csv", 1)

        chunks = [("a", i) for i in range(NCH_A)] + [("e", i) for i in range(NCH_E)]

        def proj(t, dst, wslice, corr_sl, n):
            for ft in range(NFT):
                t.matmul(dst, self.wkvq[:, ft * 384 + wslice:ft * 384 + wslice + 128],
                         self.xbf[:, CH * ft:CH * (ft + 1)],
                         start=(ft == 0), stop=False)
            return t.matmul(dst, self.csumr_b[0:1, corr_sl:corr_sl + 128],
                            self.srow_b[0:1, 0:CH], start=False, stop=True)

        with nc.Block() as blk:
            @blk.sync
            def _(sy):
                for n, (kind, i) in enumerate(chunks):
                    src = dram["xT"] if kind == "a" else dram["xTe"]
                    if n >= 2:
                        self.w(sy, "cvt", n - 1)
                    for ft in range(NFT):
                        self.inc(sy.dma_start(
                            out=self.xs[n % 2][:, CH * ft:CH * (ft + 1)],
                            in_=src[128 * ft:128 * (ft + 1), CH * i:CH * (i + 1)]),
                            f"xld{n % 2}", 16)
                    if kind == "e":
                        self.w(sy, "rrE", i + 1)
                        if i >= 1:
                            self.w(sy, "qpl", i)
                        self.inc(sy.dma_start(
                            out=self.r_row_e[0:1, 0:CH],
                            in_=self.rows64[0:4, :]), "rfl", 16)

            @blk.gpsimd
            def _(g):
                for n, (kind, i) in enumerate(chunks):
                    self.w(g, "cvt", n + 1)
                    self.inc(g.tensor_tensor(self.x2bf[:, 0:NFT * CH],
                                             self.xbf[:, 0:NFT * CH],
                                             self.xbf[:, 0:NFT * CH], OP.mult),
                             "sq", 1)
                    if kind == "e":
                        self.w(g, "srowE", n + 1)
                        self.inc(g.dma_start(out=self.sxrow_e[i:i + 1, 0:CH],
                                             in_=self.srow[0:1, 0:CH]), "sxw", 16)

            @blk.vector
            def _(v):
                self.w(v, "csv", 1)
                self.w(v, "initv", 1)
                for n, (kind, i) in enumerate(chunks):
                    self.w(v, f"xld{n % 2}", 16 * NFT * (n // 2 + 1))
                    for ft in range(NFT):
                        ins = v.tensor_copy(self.xbf[:, CH * ft:CH * (ft + 1)],
                                            self.xs[n % 2][:, CH * ft:CH * (ft + 1)])
                    self.inc(ins, "cvt", 1)
                    self.w(v, "smm", 2 * (n + 1))
                    if n >= 1 and chunks[n - 1][0] == "e":
                        self.w(v, "sxw", 16 * chunks[n - 1][1] + 16)
                    self.inc(v.tensor_copy(self.srow[:, :], psA[0:1, 0:1024]), "vch", 1)
                    self.w(v, "vch")
                    v.tensor_copy(self.srow_b[:, :], self.srow[:, :])
                    self.inc(v.engine_nop(), "srowE", 1)
                    self.w(v, "stxT", n + 1)
                    self.inc(v.tensor_scalar(self.mucol[:, :], psB[:, 0:4], 1.0 / DM,
                                             None, OP.mult), "vch", 1)
                    self.w(v, "vch")
                    self.inc(v.tensor_tensor(self.m2col[:, :], self.mucol[:, :],
                                             self.mucol[:, :], OP.mult), "vsm1", 1)
                    self.w(v, "vsm1")
                    self.inc(v.scalar_tensor_tensor(
                        self.varcol[:, :], psB[:, 4:8], 1.0 / DM,
                        self.m2col[:, :], OP.mult, OP.subtract), "vsm2", 1)
                    self.w(v, "ssm", n + 1)
                    rcol = self.rcol_a if kind == "a" else self.rcol_e
                    self.inc(v.reciprocal(rcol[:, 4 * i:4 * i + 4],
                                          self.sdcol[:, :]), "vsm3", 1)
                    self.w(v, "vsm3")
                    if kind == "a":
                        self.inc(v.tensor_scalar(
                            self.rtcol_a[:, 4 * i:4 * i + 4],
                            rcol[:, 4 * i:4 * i + 4], TOP_SCALE, None, OP.mult),
                            "smalls", 1)
                    else:
                        self.inc(v.engine_nop(), "smalls", 1)
                    self.w(v, "pmm", n + 1)
                    if kind == "a":
                        self.inc(v.tensor_copy(
                            self.kT_top[:, CH * i:CH * (i + 1)], psA[:, 1024:1536]),
                            "kevac", 1)
                        self.inc(v.tensor_copy(self.vraw[:, :], psA[:, 1536:2048]),
                                 "vevac", 1)
                        self.w(v, "vtrT", n + 1)
                        for gx in range(4):
                            tb = 132 * (4 * i + gx)
                            ins = v.tensor_scalar(
                                self.vTM_top.ap()[:, tb:tb + 132].rearrange(
                                    "p (h x) -> p h x", h=2)[:, :, 0:64],
                                psBb[:, 1024 + 128 * gx:1024 + 128 * (gx + 1)]
                                .rearrange("p (h x) -> p h x", h=2),
                                self.rcol_a[:, 4 * i + gx:4 * i + gx + 1],
                                None, OP.mult)
                        self.inc(ins, "vtme", 1)
                    else:
                        self.w(v, "rrT", i + 1)
                        self.w(v, "rfl", 16 * i)
                        self.inc(v.tensor_copy(self.rows64[0:4, :],
                                               psB[:, 512:1024][0:4, 0:128]), "rrE", 1)
                        self.w(v, "qpl", i + 1)
                        self.inc(v.tensor_copy(self.plane[:, :], psB[:, 1536:2048]),
                                 "qple", 1)
                        self.w(v, "qple")
                        self.inc(v.tensor_tensor(
                            self.qT_top[:, CH * i:CH * (i + 1)], psB[:, 1024:1536],
                            self.plane[:, :], OP.mult), "qevac", 1)

            @blk.scalar
            def _(sc):
                for n, (kind, i) in enumerate(chunks):
                    self.w(sc, "vsm2", n + 1)
                    if n >= 1:
                        self.w(sc, "vsm3", n)
                    self.inc(sc.activation(self.sdcol[:, :], self.varcol[:, :],
                                           AF.Sqrt, bias=self.epscol[:, 0:1]), "ssm", 1)

            @blk.tensor
            def _(t):
                for n, (kind, i) in enumerate(chunks):
                    self.w(t, "cvt", n + 1)
                    self.w(t, "sq", n + 1)
                    if n >= 1:
                        self.w(t, "srowE", n)
                    for ft in range(NFT):
                        ins = t.matmul(psA[0:1, 0:512], self.onesb[:, 0:1],
                                       self.xbf[:, CH * ft:CH * (ft + 1)],
                                       start=(ft == 0), stop=(ft == NFT - 1))
                    self.inc(ins, "smm", 1)
                    for ft in range(NFT):
                        ins = t.matmul(psA[0:1, 512:1024], self.onesb[:, 0:1],
                                       self.x2bf[:, CH * ft:CH * (ft + 1)],
                                       start=(ft == 0), stop=(ft == NFT - 1))
                    self.inc(ins, "smm", 1)
                    self.w(t, "srowE", n + 1)
                    if n >= 1:
                        self.w(t, "vsm2", n)
                    for gx in range(4):
                        t.transpose(psB[:, gx:gx + 1],
                                    self.srow[0:1, 128 * gx:128 * (gx + 1)],
                                    self.identf[0:1, 0:1])
                        ins = t.transpose(psB[:, 4 + gx:4 + gx + 1],
                                    self.srow[0:1, 512 + 128 * gx:512 + 128 * (gx + 1)],
                                    self.identf[0:1, 0:1])
                    self.inc(ins, "stxT", 1)
                    if kind == "a":
                        if n >= 1:
                            self.w(t, "kevac", n)
                            self.w(t, "vevac", n)
                        proj(t, psA[:, 1024:1536], 128, 128, n)
                        ins = proj(t, psA[:, 1536:2048], 256, 256, n)
                        self.inc(ins, "pmm", 1)
                        self.w(t, "vevac", n + 1)
                        if n >= 1:
                            self.w(t, "vtme", n)
                        for gx in range(4):
                            ins = t.transpose(psBb[:, 1024 + 128 * gx:1024 + 128 * (gx + 1)],
                                        self.vraw[:, 128 * gx:128 * (gx + 1)],
                                        self.identb[:, :])
                        self.inc(ins, "vtrT", 1)
                    else:
                        if i >= 1:
                            self.w(t, "qevac", i)
                        ins = proj(t, psB[:, 1024:1536], 0, 0, n)
                        self.inc(ins, "pmm", 1)
                        self.w(t, "vsm3", n + 1)
                        if i >= 1:
                            self.w(t, "rrE", i)
                        ins = t.transpose(psB[:, 512:1024][0:4, 0:128],
                                    self.rcol_e[:, 4 * i:4 * i + 4],
                                    self.identf[:, :])
                        self.inc(ins, "rrT", 1)
                        self.w(t, "rfl", 16 * (i + 1))
                        if i >= 1:
                            self.w(t, "qple", i)
                        ins = t.matmul(psB[:, 1536:2048], self.onesrow[0:1, 0:128],
                                 self.r_row_e[0:1, 0:CH],
                                 start=True, stop=True)
                        self.inc(ins, "qpl", 1)

        if self.dbg:
            self._dbg_dump_kvq()

    def _dbg_dump_kvq(self):
        nc = self.nc
        kd = self.dout("dbg_kT", (128, SEQ), BF16)
        vd = self.dout("dbg_vTM", (128, NG_A * 128), BF16)
        qd = self.dout("dbg_qT", (128, EXT), BF16)
        rd = self.dout("dbg_rt", (128, NG_A))
        self.dbg_names += ["dbg_kT", "dbg_vTM", "dbg_qT", "dbg_rt"]
        with nc.Block() as blk:
            @blk.gpsimd
            def _(g):
                self.w(g, "vtme")
                self.w(g, "qevac")
                self.w(g, "smalls")
                self.inc(g.dma_start(out=kd, in_=self.kT_top[:, :]), "dbgst", 16)
                self.inc(g.dma_start(out=vd, in_=self.vTM_top[:, :]), "dbgst", 16)
                self.inc(g.dma_start(out=qd, in_=self.qT_top[:, :]), "dbgst", 16)
                self.inc(g.dma_start(out=rd, in_=self.rtcol_a[:, :]), "dbgst", 16)
                self.w(g, "dbgst")


    def phase_top_attn(self):
        nc = self.nc
        psA, psB = self.psA, self.psB
        NKT = NG_A

        with nc.Block() as blk:
            @blk.tensor
            def _(t):
                self.w(t, "qevac"); self.w(t, "vtme"); self.w(t, "smalls")
                self.w(t, "vtmo", 1)
                SSLOT = (psB.ap()[:, 0:512], psB.ap()[:, 512:1024],
                         psA.ap()[:, 1024:1536], psA.ap()[:, 1536:2048])
                for q4 in range(NCH_E):
                    for kt in range(NKT):
                        g = q4 * NKT + kt
                        psS = SSLOT[kt % 4]
                        if g >= 4:
                            self.w(t, "texp", g - 3)
                        t.matmul(psS, self.kT_top[:, 128 * kt:128 * (kt + 1)],
                                 self.qT_top[:, CH * q4:CH * (q4 + 1)],
                                 start=True, stop=True).then_inc(self.sem("smmT"), 1)
                        self.cnt["smmT"] += 1
                        if kt >= 2:
                            kp = kt - 2
                            self.w(t, "texp", q4 * NKT + kp + 1)
                            if kp == 0 and q4 >= 1:
                                self.w(t, "oev", q4)
                            for h in range(2):
                                ins = t.matmul(
                                    psA[0:65, 512 * h:512 * h + 512],
                                    self.vTM_top[:, 132 * kp + 66 * h:
                                                 132 * kp + 66 * h + 65],
                                    self.pb[kp % 4][:, :],
                                    start=(kp == 0), stop=False)
                            self.inc(ins, "pvT", 1)
                    for kp in (NKT - 2, NKT - 1):
                        self.w(t, "texp", q4 * NKT + kp + 1)
                        for h in range(2):
                            ins = t.matmul(psA[0:65, 512 * h:512 * h + 512],
                                           self.vTM_top[:, 132 * kp + 66 * h:
                                                        132 * kp + 66 * h + 65],
                                           self.pb[kp % 4][:, :],
                                           start=False, stop=(kp == NKT - 1))
                        self.inc(ins, "pvT", 1)
                    self.w(t, "rde", q4 + 1)
                    if q4 >= 1:
                        self.w(t, "ohv", q4)
                    ins = t.matmul(psB[:, 1536:2048], self.onesb[0:1, 0:128],
                                   self.rdrowb[0:1, :], start=True, stop=True)
                    self.inc(ins, "rplT", 1)

            @blk.scalar
            def _(sc):
                SSLOT = (psB.ap()[:, 0:512], psB.ap()[:, 512:1024],
                         psA.ap()[:, 1024:1536], psA.ap()[:, 1536:2048])
                for q4 in range(NCH_E):
                    for kt in range(NKT):
                        g = q4 * NKT + kt
                        self.w(sc, "smmT", g + 1)
                        if g >= 4:
                            self.w(sc, "pvT", g - 3)
                        self.inc(sc.activation(
                            self.pb[kt % 4][:, :], SSLOT[kt % 4],
                            AF.Exp, scale=self.rtcol_a[:, kt:kt + 1]), "texp", 1)

            @blk.vector
            def _(v):
                self.inc(v.memset(self.vTM_top.ap()[:, 64:NG_A * 132:66], 1.0),
                         "vtmo", 1)
                for q4 in range(NCH_E):
                    self.w(v, "pvT", (q4 + 1) * NKT)
                    if q4 >= 1:
                        self.w(v, "ohv", q4)
                    v.tensor_copy(self.osb[0:64, :], psA[0:64, 0:512])
                    v.tensor_copy(self.osb[64:128, :], psA[0:64, 512:1024])
                    self.inc(v.tensor_copy(self.rdrowb[0:1, :],
                                           psA[64:65, 0:512]), "rde", 1)
                    self.inc(v.engine_nop(), "oev", 1)
                    self.w(v, "rplT", q4 + 1)
                    self.inc(v.reciprocal(psB[:, 1024:1536], psB[:, 1536:2048]),
                             "vch", 1)
                    self.w(v, "vch")
                    self.inc(v.tensor_tensor(self.oh_top[:, CH * q4:CH * (q4 + 1)],
                                             self.osb[:, :], psB[:, 1024:1536],
                                             OP.mult), "ohv", 1)

    def phase_top_merge(self):
        nc = self.nc
        dram = self.dram
        psA, psB = self.psA, self.psB

        with nc.Block() as blk:
            @blk.gpsimd
            def _(g):
                self.inc(g.dma_start(out=self.wo_b[:, :],
                                     in_=dram["wo_top"]), "woL", 16)

            @blk.scalar
            def _(sc):
                self.w(sc, "woL", 16)
                self.w(sc, "initv", 1)
                self.inc(sc.activation(self.wo_b[:, :], self.wo_b[:, :],
                                       AF.Copy, scale=self.mask_f[:, 0:1]), "wob", 1)

            @blk.vector
            def _(v):
                self.w(v, "wob", 1)
                self.inc(v.reduce_sum(self.rwcol[:, 0:1], self.wo_b[:, :], AX.X),
                         "vch", 1)
                self.w(v, "vch")
                self.inc(v.tensor_scalar(self.rwcol_b[:, 0:1], self.rwcol[:, 0:1],
                                         -1.0 / DM, None, OP.mult), "rwb", 1)

        base_xld = [self.cnt.get("xld0", 0), self.cnt.get("xld1", 0)]
        with nc.Block() as blk:
            @blk.sync
            def _(sy):
                for c in range(NCH_E):
                    if c >= 2:
                        self.w(sy, "x1v", NFT * (c - 1))
                    for ft in range(NFT):
                        self.inc(sy.dma_start(
                            out=self.xs[c % 2][:, CH * ft:CH * (ft + 1)],
                            in_=dram["xTe"][128 * ft:128 * (ft + 1), CH * c:CH * (c + 1)]),
                            f"xld{c % 2}", 16)

            @blk.tensor
            def _(t):
                self.w(t, "rwb", 1)
                self.w(t, "ohv")
                self.w(t, "texp")
                for c in range(NCH_E):
                    if c >= 1:
                        self.w(t, "syE", c)
                    ins = t.matmul(psA[0:1, 1024:1536], self.rwcol_b[:, 0:1],
                                   self.oh_top[:, CH * c:CH * (c + 1)],
                                   start=True, stop=True)
                    self.inc(ins, "syT", 1)
                    self.w(t, "syE2", c + 1)
                    for fc in range(NFT):
                        if c * NFT + fc >= 2:
                            self.w(t, "yev", c * NFT + fc - 1)
                        psY = psA[:, 512 * (fc % 2):512 * (fc % 2) + 512]
                        t.matmul(psY, self.wo_b[:, 128 * fc:128 * (fc + 1)],
                                 self.oh_top[:, CH * c:CH * (c + 1)],
                                 start=True, stop=False)
                        ins = t.matmul(psY, self.onesb[0:1, 0:128],
                                       self.nmy_b[0:1, :], start=False, stop=True)
                        self.inc(ins, "ymm", 1)
                    self.w(t, "ysq", c + 1)
                    if c >= 1:
                        self.w(t, "ys2E", c)
                    for ft in range(NFT):
                        ins = t.matmul(psB[0:1, 0:512], self.onesb[:, 0:1],
                                       self.x2bf[:, CH * ft:CH * (ft + 1)],
                                       start=(ft == 0), stop=(ft == NFT - 1))
                    self.inc(ins, "ys2T", 1)
                    self.w(t, "yst", c + 1)
                    if c >= 1:
                        self.w(t, "x1v", NFT * c)
                    ins = t.matmul(psB[:, 1024:1536], self.onesrow[0:1, 0:128],
                                   self.yrows[0:1, 4 * CH:5 * CH],
                                   start=True, stop=True)
                    self.inc(ins, "yplT", 1)
                    self.w(t, "x1sq", c + 1)
                    if c >= 1:
                        self.w(t, "x12E", c)
                    for ft in range(NFT):
                        ins = t.matmul(psB[0:1, 512:1024], self.onesb[:, 0:1],
                                       self.x2bf[:, CH * ft:CH * (ft + 1)],
                                       start=(ft == 0), stop=(ft == NFT - 1))
                    self.inc(ins, "x12T", 1)

            @blk.vector
            def _(v):
                for c in range(NCH_E):
                    self.w(v, "syT", c + 1)
                    self.inc(v.tensor_copy(self.nmy[:, :], psA[0:1, 1024:1536]),
                             "vch", 1)
                    self.w(v, "vch")
                    self.inc(v.tensor_copy(self.nmy_b[:, :], self.nmy[:, :]), "syE2", 1)
                    self.inc(v.engine_nop(), "syE", 1)
                    if c >= 1:
                        self.w(v, "ysq", c)
                    for fc in range(NFT):
                        self.w(v, "ymm", c * NFT + fc + 1)
                        self.inc(v.tensor_copy(
                            self.ybf[:, CH * fc:CH * (fc + 1)],
                            psA[:, 512 * (fc % 2):512 * (fc % 2) + 512]), "yev", 1)
                    self.w(v, "ys2T", c + 1)
                    self.inc(v.tensor_scalar(self.yrows[0:1, 0:CH], psB[0:1, 0:512],
                                              1.0 / DM, None, OP.mult), "ys2E", 1)
                    self.w(v, "ysd", c + 1)
                    self.inc(v.reciprocal(self.yrows[0:1, 4 * CH:5 * CH],
                                          self.yrows[0:1, 3 * CH:4 * CH]), "yst", 1)
                    self.w(v, "yplT", c + 1)
                    self.w(v, f"xld{c % 2}", base_xld[c % 2] + 16 * NFT * (c // 2 + 1))
                    if c >= 1:
                        self.w(v, "x1sq", c)
                        self.w(v, "x1w", 16 * NFT * c)
                    for fc in range(NFT):
                        self.inc(v.tensor_tensor(
                            self.utile[:, :],
                            self.ybf[:, CH * fc:CH * (fc + 1)],
                            psB[:, 1024:1536], OP.mult), "vch", 1)
                        self.w(v, "vch")
                        self.inc(v.tensor_add(
                            self.x1s[:, CH * fc:CH * (fc + 1)],
                            self.xs[c % 2][:, CH * fc:CH * (fc + 1)],
                            self.utile[:, :]), "vch", 1)
                        self.w(v, "vch")
                        self.inc(v.tensor_copy(self.xbf[:, CH * fc:CH * (fc + 1)],
                                               self.x1s[:, CH * fc:CH * (fc + 1)]),
                                 "x1v", 1)
                    self.w(v, "x1sq", c + 1)
                    self.w(v, "x12T", c + 1)
                    self.inc(v.tensor_scalar(self.yrows[0:1, 0:CH],
                                             psB[0:1, 512:1024],
                                             1.0 / DM, None, OP.mult), "x12E", 1)
                    self.w(v, "sxr", 16 * (c + 1))
                    self.inc(v.tensor_scalar(self.yrows[0:1, CH:2 * CH],
                                             self.sxstg[0:1, 0:CH],
                                             1.0 / DM, None, OP.mult), "mu1v", 1)
                    self.w(v, "mu1v")
                    self.inc(v.tensor_tensor(self.yrows[0:1, 2 * CH:3 * CH],
                                             self.yrows[0:1, CH:2 * CH],
                                             self.yrows[0:1, CH:2 * CH], OP.mult),
                             "vch", 1)
                    self.w(v, "vch")
                    self.inc(v.tensor_sub(self.yrows[0:1, 0:CH],
                                          self.yrows[0:1, 0:CH],
                                          self.yrows[0:1, 2 * CH:3 * CH]), "x1v1", 1)
                    self.w(v, "x1sd", c + 1)
                    if c >= 1:
                        self.w(v, "x1st", 32 * c)
                    self.inc(v.reciprocal(self.rstg[0:1, 0:CH],
                                          self.yrows[0:1, 3 * CH:4 * CH]), "vch", 1)
                    self.w(v, "vch")
                    self.inc(v.tensor_tensor(self.yrows[0:1, 5 * CH:6 * CH],
                                             self.yrows[0:1, CH:2 * CH],
                                             self.rstg[0:1, 0:CH],
                                             OP.mult), "vch", 1)
                    self.w(v, "vch")
                    self.inc(v.tensor_scalar(self.nstg[0:1, 0:CH],
                                             self.yrows[0:1, 5 * CH:6 * CH],
                                             -1.0, None, OP.mult), "x1stv", 1)

            @blk.scalar
            def _(sc):
                for c in range(NCH_E):
                    self.w(sc, "yev", (c + 1) * NFT)
                    for ft in range(NFT):
                        ins = sc.activation(self.x2bf[:, CH * ft:CH * (ft + 1)],
                                            self.ybf[:, CH * ft:CH * (ft + 1)],
                                            AF.Square)
                    self.inc(ins, "ysq", 1)
                    self.w(sc, "ys2E", c + 1)
                    self.inc(sc.activation(self.yrows[0:1, 3 * CH:4 * CH],
                                           self.yrows[0:1, 0:CH],
                                           AF.Sqrt, bias=self.epscol[0:1, 0:1]),
                             "ysd", 1)
                    self.w(sc, "x1v", (c + 1) * NFT)
                    self.w(sc, "ys2T", c + 1)
                    for ft in range(NFT):
                        ins = sc.activation(self.x2bf[:, CH * ft:CH * (ft + 1)],
                                            self.xbf[:, CH * ft:CH * (ft + 1)],
                                            AF.Square)
                    self.inc(ins, "x1sq", 1)
                    self.w(sc, "x1v1", c + 1)
                    self.inc(sc.activation(self.yrows[0:1, 3 * CH:4 * CH],
                                           self.yrows[0:1, 0:CH],
                                           AF.Sqrt, bias=self.epscol[0:1, 0:1]),
                             "x1sd", 1)

            @blk.gpsimd
            def _(g):
                for c in range(NCH_E):
                    if c == 0:
                        self.w(g, "sxw")
                    if c >= 1:
                        self.w(g, "mu1v", c)
                    self.inc(g.dma_start(out=self.sxstg[0:1, 0:CH],
                                         in_=self.sxrow_e[c:c + 1, 0:CH]), "sxr", 16)
                    self.w(g, "x1v", (c + 1) * NFT)
                    for ft in range(NFT):
                        self.inc(g.dma_start(
                            out=self.dram["st_x1"][128 * ft:128 * (ft + 1),
                                                   CH * c:CH * (c + 1)],
                            in_=self.x1s[:, CH * ft:CH * (ft + 1)]), "x1w", 16)
                    self.w(g, "x1stv", c + 1)
                    self.inc(g.dma_start(out=self.r1row[c:c + 1, 0:CH],
                                         in_=self.rstg[0:1, 0:CH]), "x1st", 16)
                    self.inc(g.dma_start(out=self.nmr1row[c:c + 1, 0:CH],
                                         in_=self.nstg[0:1, 0:CH]), "x1st", 16)

        with nc.Block() as blk:
            @blk.sync
            def _(sy):
                self.inc(sy.dma_start(out=self.tvrsb[:, :], in_=dram["tvr"]),
                         "tvl", 16)

            @blk.vector
            def _(v):
                self.w(v, "tvl", 16)
                self.w(v, "x1st")
                self.inc(v.tensor_tensor(self.r1row[:, :], self.r1row[:, :],
                                         self.tvrsb[:, :], OP.mult), "vch", 1)
                self.w(v, "vch")
                self.inc(v.tensor_tensor(self.nmr1row[:, :],
                                         self.nmr1row[:, :],
                                         self.tvrsb[:, :], OP.mult), "tvv", 1)
                self.w(v, "tvv", 1)

    def _dbg_dump_merge(self):
        nc = self.nc
        od = self.dout("dbg_oh", (128, EXT), BF16)
        r1d = self.dout("dbg_r1", (NCH_E, CH))
        n1d = self.dout("dbg_nmr1", (NCH_E, CH))
        self.dbg_names += ["dbg_oh", "dbg_r1", "dbg_nmr1"]
        with nc.Block() as blk:
            @blk.gpsimd
            def _(g):
                self.w(g, "ohv")
                self.w(g, "x1st")
                self.inc(g.dma_start(out=od, in_=self.oh_top[:, :]), "dbgst2", 16)
                self.inc(g.dma_start(out=r1d, in_=self.r1row[:, :]), "dbgst2", 16)
                self.inc(g.dma_start(out=n1d, in_=self.nmr1row[:, :]), "dbgst2", 16)
                self.w(g, "dbgst2")


    def declare_inner(self):
        self.st2 = contextlib.ExitStack()
        s2 = lambda n, sh, dt=F32: self.sbuf(n, sh, dt, st=self.st2)
        self.wqkvb = s2("wqkvb", [128, NFT * 1536], BF16)
        self.wob2 = s2("wob2", [128, 4 * DM], BF16)
        self.qhat = s2("qhat", [128, 4 * CH], BF16)
        self.khat = s2("khat", [128, 2 * 4 * CH], BF16)
        self.vtm = s2("vtm", [128, 2 * 4 * 528], BF16)
        self.ohU = s2("ohU", [128, 4 * CH], BF16)
        self.xt2 = s2("xt2", [128, NFT * CH], BF16)
        self.tabs = [s2("tabs0", [128, 4 * CH], BF16)] * 2
        self.w1s = [s2(f"w1s{i}", [128, 2 * NFT * 128], BF16) for i in range(2)]
        self.w2s = [s2(f"w2s{i}", [128, DM], BF16) for i in range(2)]
        self.hbuf = s2("hbuf", [128, 22 * 512], BF16)
        self.pbL = s2("pbL", [128, 4096], BF16)
        self.q2b = s2("q2b", [128, CH], BF16)
        self.t1b = s2("t1b", [128, CH], BF16)
        self.ab = s2("ab", [128, CH], BF16)
        self.bb = s2("bb", [128, CH], BF16)
        self.rqrows = s2("rqrows", [2, CH], BF16)
        self.sd8 = s2("sd8", [128, 8])
        self.rc8 = s2("rc8", [128, 8])
        self.rc8b = s2("rc8b", [128, 8], BF16)
        self.gelub = s2("gelub", [128, 512], BF16)
        self.drcpb = s2("drcpb", [1, 2 * CH], BF16)
        self.halocols = s2("halocols", [128, NFT])
        self.halocolsF = s2("halocolsF", [128, NFT])
        self.bheadc = s2("bheadc", [128, 2], BF16)
        self.bheadb = s2("bheadb", [2, 128], BF16)
        self.psw = s2("psw", [128, 128], BF16)
        self.trim = s2("trim", [128, 256], BF16)
        self.kvc = s2("kvc", [128, 28])
        self.qscol = s2("qscol", [128, 1]); self.qsswcol = s2("qsswcol", [128, 1])
        self.kscol = s2("kscol", [128, 1]); self.ksswcol = s2("ksswcol", [128, 1])
        self.b1acol = s2("b1acol", [128, 22]); self.b1gcol = s2("b1gcol", [128, 22])
        self.b2col = s2("b2col", [128, NFT])
        self.gcol = s2("gcol", [128, NFT]); self.bcol = s2("bcol", [128, NFT])

    def phase_wprep(self, l):
        nc = self.nc
        dram = self.dram
        psA, psB = self.psA, self.psB
        L = str(l)
        P = ""

        with nc.Block() as blk:
            @blk.sync
            def _(sy):
                if l == 0:
                    for nm, t in (("psw", self.psw), ("trimasks", self.trim),
                                  ("kvc", self.kvc)):
                        self.inc(sy.dma_start(out=t[:, :], in_=dram[nm]), P + "cld", 16)

            @blk.gpsimd
            def _(g):
                self.inc(g.dma_start(
                    out=self.wqkvb[:, :],
                    in_=dram["wqkv" + L].rearrange("(a p) c -> p a c", p=128)),
                    P + "wqk", 16)
                self.inc(g.dma_start(
                    out=self.wob2[:, :],
                    in_=dram["wo" + L].rearrange("(a p) c -> p a c", p=128)),
                    P + "wobi", 16)
                self.inc(g.dma_start(out=self.dram["w1bf"], in_=dram["w1" + L]),
                         P + "w1w", 64)
                self.inc(g.dma_start(out=self.dram["w2bf"], in_=dram["w2" + L]),
                         P + "w2w", 16)

        self._wprep_cols(l)

    def _wprep_cols(self, l):
        nc = self.nc
        dram = self.dram
        psA, psB = self.psA, self.psB
        L = str(l)
        P = ""
        jobs = []
        for j in range(22):
            w = 128 if j < 21 else 42
            jobs.append(("b1" + L, 128 * j, w, "b1acol", j))
            jobs.append(("b1" + L, HID + 128 * j, w, "b1gcol", j))
        for fc in range(NFT):
            jobs.append(("b2" + L, 128 * fc, 128, "b2col", fc))
            jobs.append(("lng" + L, 128 * fc, 128, "gcol", fc))
            jobs.append(("lnb" + L, 128 * fc, 128, "bcol", fc))
        jobs.append(("qs" + L, 0, 64, "qsraw", 0))
        jobs.append(("ks" + L, 0, 64, "ksraw", 0))
        dest = {"b1acol": self.b1acol, "b1gcol": self.b1gcol,
                "b2col": self.b2col, "gcol": self.gcol, "bcol": self.bcol,
                "qsraw": self.qscol, "ksraw": self.kscol}

        with nc.Block() as blk:
            @blk.sync
            def _(sy):
                self.inc(sy.dma_start(out=self.bheadb[:, :],
                                      in_=dram["bheadb_c"]), P + "crl", 16)
                for n, (dn, col, w, dt_, dc) in enumerate(jobs):
                    if n >= 1:
                        self.w(sy, P + "ctx", n)
                    self.inc(sy.dma_start(out=self.srow[0:1, 0:w],
                                          in_=dram[dn][0:1, col:col + w]),
                             P + "crl", 16)

            @blk.tensor
            def _(t):
                for n, (dn, col, w, dt_, dc) in enumerate(jobs):
                    self.w(t, P + "crl", 16 * (n + 2))
                    if n >= 2:
                        self.w(t, P + "cev", n - 1)
                    ins = t.transpose(psB[:, (n % 2):(n % 2) + 1][0:w, 0:1],
                                      self.srow[0:1, 0:w], self.identf[0:1, 0:1])
                    self.inc(ins, P + "ctx", 1)

            @blk.vector
            def _(v):
                for n, (dn, col, w, dt_, dc) in enumerate(jobs):
                    self.w(v, P + "ctx", n + 1)
                    d = dest[dt_]
                    ins = v.tensor_copy(d[0:w, dc:dc + 1],
                                        psB[:, (n % 2):(n % 2) + 1][0:w, 0:1])
                    self.inc(ins, P + "cev", 1)
                self.w(v, P + "cev", len(jobs))
                self.inc(v.tensor_copy(self.qscol[64:128, 0:1], self.qscol[0:64, 0:1]), "vch", 1)
                self.inc(v.tensor_copy(self.kscol[64:128, 0:1], self.kscol[0:64, 0:1]), "vch", 1)
                self.w(v, "vch")
                for srcc, dstc in ((self.qscol, self.qsswcol), (self.kscol, self.ksswcol)):
                    v.tensor_copy(dstc[0:32, 0:1], srcc[32:64, 0:1])
                    v.tensor_copy(dstc[32:64, 0:1], srcc[0:32, 0:1])
                    v.tensor_copy(dstc[64:96, 0:1], srcc[96:128, 0:1])
                    ins = v.tensor_copy(dstc[96:128, 0:1], srcc[64:96, 0:1])
                self.inc(ins, P + "colsv", 1)
                self.inc(v.memset(self.bheadc[:, :], 0.0), "vch", 1)
                self.w(v, "vch")
                v.memset(self.bheadc[0:64, 0:1], 1.0)
                self.inc(v.memset(self.bheadc[64:128, 1:2], 1.0), "vch", 1)
                self.inc(v.memset(self.vtm.ap()[:, 64:2 * 4 * 528:66], 1.0),
                         P + "colsv", 1)
            @blk.scalar
            def _(sc):
                pass


    def phase_inner(self, l):
        nc = self.nc
        dram = self.dram
        psA, psB = self.psA, self.psB
        psBb = psB.ap().bitcast(BF16)
        P = ""
        NCH = NCH_E
        src = dram["st_x1"] if l == 0 else dram["st_in"]
        tabnames = ("csq", "ssq", "csk", "ssk")
        Xbf = self.xbf
        BCH = list(range(0, NCH)) if l == 0 else list(range(1, NCH))
        EFC = list(range(1, NCH)) if l == 0 else list(range(2, NCH))
        GIC = list(range(1, NCH)) if l == 0 else list(range(3, NCH))

        def idx(c):
            bi = BCH.index(c)
            inEF = c in EFC
            ei = EFC.index(c) if inEF else None
            inGI = c in GIC
            zi = GIC.index(c) if inGI else None
            nef = sum(1 for x in EFC if x < c)
            ngi = sum(1 for x in GIC if x < c)
            bp = sum(1 for x in BCH if x < c and x % 2 == c % 2)
            nefp = sum(1 for x in EFC if x < c and x % 2 == c % 2)
            return bi, inEF, ei, inGI, zi, nef, ngi, bp, nefp
        NEG12 = 1e-12

        with nc.Block() as blk:
            @blk.sync
            def _(sy):
                for c in BCH:
                    bi, inEF, ei, inGI, zi, nef, ngi, bp, nefp = idx(c)
                    if bi >= 2:
                        self.w(sy, P + f"inw{c % 2}", 16 * nefp)
                        self.w(sy, P + "xbv", NFT * (bi - 1))
                    self.inc(sy.dma_start(
                        out=self.xs[c % 2].ap().rearrange(
                            "p (a c) -> p a c", a=NFT),
                        in_=src[:, CH * c:CH * (c + 1)].rearrange(
                            "(a p) c -> p a c", p=128)),
                        P + f"ild{c % 2}", 16)
                    if bi >= 1:
                        self.w(sy, P + "qkv2", 8 * bi)
                    for tb in range(4):
                        self.inc(sy.dma_start(
                            out=self.tabs[c % 2][:, CH * tb:CH * (tb + 1)],
                            in_=dram[tabnames[tb]][:, CH * c:CH * (c + 1)]),
                            P + f"tld{c % 2}", 16)
                    if not inGI:
                        continue
                    for j in range(22):
                        g = 22 * zi + j
                        wj = 128 if j < 21 else 42
                        off = 256 * j if j < 21 else 5376
                        if g >= 2:
                            self.w(sy, P + "hT", g - 1)
                        else:
                            self.w(sy, P + "w1w", 64)
                        self.inc(sy.dma_start(
                            out=self.w1s[j % 2][:, 0:16 * wj],
                            in_=dram["w1bf"][:, off:off + 2 * wj].rearrange(
                                "(a p) c -> p a c", p=128)),
                            P + f"w1ld{j % 2}", 16)
                    for kt in range(22):
                        g = 22 * zi + kt
                        r = 128 if kt < 21 else 42
                        if g >= 2:
                            self.w(sy, P + "iT", g - 1)
                        else:
                            self.w(sy, P + "w2w", 16)
                        self.inc(sy.dma_start(
                            out=self.w2s[kt % 2][0:r, :],
                            in_=dram["w2bf"][128 * kt:128 * kt + r, :]),
                            P + f"w2ld{kt % 2}", 16)

            @blk.gpsimd
            def _(g):
                for c in BCH:
                    bi, inEF, ei, inGI, zi, nef, ngi, bp, nefp = idx(c)
                    if l == 0:
                        if bi >= 1:
                            self.w(g, P + "aplT", 2 * bi)
                        self.inc(g.dma_start(out=self.rstg[0:1, 0:CH],
                                             in_=self.r1row[c:c + 1, 0:CH]),
                                 P + "rst", 16)
                        self.inc(g.dma_start(out=self.nstg[0:1, 0:CH],
                                             in_=self.nmr1row[c:c + 1, 0:CH]),
                                 P + "rst", 16)
                    if self.dbg and l == 0 and inEF:
                        self.w(g, P + "xmbf", NFT * (ei + 1))
                        self.inc(g.dma_start(
                            out=self.dram.get("dbgM")[:, 4224 * c:4224 * (c + 1)],
                            in_=Xbf[:, :]), "dbgMw", 16)
                    if inEF:
                        self.w(g, P + "outv", NFT * (ngi + (1 if inGI else 0)))
                        self.w(g, P + "xmidv", NFT * (ei + 1))
                        self.inc(g.dma_start(
                            out=dram["st_in"][:, CH * c:CH * (c + 1)].rearrange(
                                "(a p) c -> p a c", p=128),
                            in_=self.xs[c % 2].ap().rearrange(
                                "p (a c) -> p a c", a=NFT)),
                            P + f"inw{c % 2}", 16)

            @blk.tensor
            def _(t):
                if l == 0:
                    self.w(t, "cld")
                for c in BCH:
                    bi, inEF, ei, inGI, zi, nef, ngi, bp, nefp = idx(c)
                    ov0 = NFT * (ngi - 1) if ngi >= 1 else 0
                    ovn = NFT if ngi >= 1 else 0
                    self.w(t, P + "geS", 22 * ngi)
                    if l == 0:
                        self.w(t, P + "rst", 32 * (bi + 1))
                        if ovn:
                            self.w(t, P + "outv", ov0 + 4)
                        if bi >= 1:
                            self.w(t, P + "xbv", NFT * bi)
                        t.matmul(psA[:, 1024:1536], self.onesrow[0:1, 0:128],
                                 self.rstg[0:1, 0:CH], start=True, stop=True)
                        ins = t.matmul(psA[:, 1536:2048], self.onesrow[0:1, 0:128],
                                       self.nstg[0:1, 0:CH], start=True, stop=True)
                        self.inc(ins, P + "aplT", 2)
                    if bi == 0:
                        self.w(t, P + "wqk", 16)
                    self.w(t, P + "harv", bi + 1)
                    self.w(t, P + "xbv", NFT * (bi + 1))
                    for oc in range(12):
                        gi = 12 * bi + oc
                        if ovn and oc < 2:
                            self.w(t, P + "outv", ov0 + (oc % 2) + 1)
                        if oc == 0:
                            self.w(t, P + "odV", 4 * nef)
                        if gi >= 2:
                            self.w(t, P + "ocv", gi - 1)
                        if 2 <= oc < 10:
                            self.w(t, P + "sqS", 8 * bi + oc - 1)
                        dst = psA[:, 512 * (oc % 2):512 * (oc % 2) + 512]
                        for kt in range(NFT):
                            sh = kt >= 4
                            b = 528 * kt + (15 if sh else 16)
                            ins = t.matmul(dst,
                                           self.wqkvb[:, 1536 * kt + 128 * oc:
                                                      1536 * kt + 128 * (oc + 1)],
                                           Xbf[:, b:b + 512],
                                           start=(kt == 0), stop=(kt == NFT - 1))
                        self.inc(ins, P + "ocmm", 1)
                        if oc < 8:
                            qki = 8 * bi + oc
                            if ovn and oc == 0:
                                self.w(t, P + "outv", ov0 + NFT)
                            self.w(t, P + "sqS", qki + 1)
                            if qki >= 1:
                                self.w(t, P + "ssV", qki)
                            for ts in range(4):
                                ins = t.matmul(psB[:, 8 * oc + 2 * ts:8 * oc + 2 * ts + 2],
                                               self.q2b[:, 128 * ts:128 * (ts + 1)],
                                               self.bheadc[:, 0:2],
                                               start=True, stop=True)
                            self.inc(ins, P + "ssT", 1)
                            self.w(t, P + "ssV2", qki + 1)
                            if qki >= 1:
                                self.w(t, P + "rqE", qki)
                            for ts in range(4):
                                ins = t.transpose(
                                    psBb[0:2, 1024 + 128 * ts:1024 + 128 * (ts + 1)],
                                    self.rc8b[:, 2 * ts:2 * ts + 2],
                                    self.identb[:, :])
                            self.inc(ins, P + "rqT", 1)
                            self.w(t, P + "rqE2", qki + 1)
                            if qki >= 1:
                                self.w(t, P + "t1v", qki)
                            ins = t.matmul(psA[:, 1536:2048], self.bheadb[:, :],
                                           self.rqrows[0:2, 0:CH],
                                           start=True, stop=True)
                            self.inc(ins, P + "plT", 1)
                            self.w(t, P + "t1v", qki + 1)
                            if qki >= 1:
                                self.w(t, P + "bv", qki)
                            ins = t.matmul(psA[:, 1024:1536], self.psw[:, :],
                                           self.t1b[:, :], start=True, stop=True)
                            self.inc(ins, P + "swT", 1)
                        else:
                            vi = 4 * bi + (oc - 8)
                            self.w(t, P + "vbE", vi + 1)
                            if vi >= 1:
                                self.w(t, P + "vtE", vi)
                            for ts in range(4):
                                ins = t.transpose(
                                    psBb[:, 2048 + 128 * ts:2048 + 128 * (ts + 1)],
                                    self.vraw[:, 128 * ts:128 * (ts + 1)],
                                    self.identb[:, :])
                            self.inc(ins, P + "vtT", 1)
                    if not inEF:
                        continue
                    if ovn:
                        self.w(t, P + "outv", ov0 + NFT)
                    self.w(t, P + "qkv2", 8 * (bi + 1))
                    self.w(t, P + "vtE", 4 * (bi + 1))
                    kks = list(range(4 * c - 4, 4 * c + 4))
                    NKK = 8
                    u0c = 32 * ei
                    def emit_plane(hp):
                        self.w(t, P + "drV", 4 * ei + hp + 1)
                        if 4 * ei + hp >= 1:
                            self.w(t, P + "normv", 4 * ei + hp)
                        t.matmul(psB[0:64, 1536:2048], self.onesb[0:1, 0:64],
                                 self.drcpb[0:1, 0:CH], start=True, stop=True)
                        ins = t.matmul(psB[64:128, 1536:2048],
                                       self.onesb[0:1, 0:64],
                                       self.drcpb[0:1, CH:2 * CH],
                                       start=True, stop=True,
                                       tile_position=(0, 64))
                        self.inc(ins, P + "dpT", 1)

                    for hp in range(4):
                        emitted_pv = 0
                        for i, kk in enumerate(kks):
                            u = u0c + hp * NKK + i
                            slot = u % 2
                            if hp >= 1 and i == 2:
                                emit_plane(hp - 1)
                            if u >= 2:
                                self.w(t, P + "exS", u - 1)
                            for h in range(2):
                                lh = self.khat[64 * h:64 * (h + 1),
                                               ((kk // 4) % 2) * 4 * CH + hp * CH +
                                               128 * (kk % 4):
                                               ((kk // 4) % 2) * 4 * CH + hp * CH +
                                               128 * (kk % 4) + 128]
                                ins = t.matmul(
                                    psA[:, 1024 * slot + 512 * h:
                                        1024 * slot + 512 * h + 512],
                                    lh, self.qhat[64 * h:64 * (h + 1),
                                                  hp * CH:hp * CH + 512],
                                    start=True, stop=True,
                                    tile_position=(64 * h, 0))
                            self.inc(ins, P + "sT", 1)
                            if i >= 2:
                                up = u - 2
                                kp = kks[i - 2]
                                self.w(t, P + "mkv", up + 1)
                                if i - 2 == 0:
                                    self.w(t, P + "odV", 4 * ei + hp)
                                for h in range(2):
                                    vb = (((kp // 4) % 2) * 4 * 528 +
                                          528 * (kp % 4) + 132 * hp + 66 * h)
                                    ins = t.matmul(
                                        psB[0:65, 512 * h:512 * h + 512],
                                        self.vtm[:, vb:vb + 65],
                                        self.pbL[:, 1024 * (up % 4) + 512 * h:
                                                 1024 * (up % 4) + 512 * h + 512],
                                        start=(i - 2 == 0), stop=False)
                                self.inc(ins, P + "pvT", 1)
                                emitted_pv += 1
                        for ii in (NKK - 2, NKK - 1):
                            u = u0c + hp * NKK + ii
                            kp = kks[ii]
                            self.w(t, P + "mkv", u + 1)
                            for h in range(2):
                                vb = (((kp // 4) % 2) * 4 * 528 +
                                      528 * (kp % 4) + 132 * hp + 66 * h)
                                ins = t.matmul(
                                    psB[0:65, 512 * h:512 * h + 512],
                                    self.vtm[:, vb:vb + 65],
                                    self.pbL[:, 1024 * (u % 4) + 512 * h:
                                             1024 * (u % 4) + 512 * h + 512],
                                    start=False, stop=(ii == NKK - 1))
                            self.inc(ins, P + "pvT", 1)
                        if hp == 3:
                            emit_plane(hp)
                    if ei == 0:
                        self.w(t, P + "wobi", 16)
                    self.w(t, P + "normv", 4 * (ei + 1))
                    for fc in range(NFT):
                        gi = NFT * ei + fc
                        if gi >= 2:
                            self.w(t, P + "xmidv", gi - 1)
                        dst = psA[:, 512 * (fc % 2):512 * (fc % 2) + 512]
                        for kt in range(4):
                            ins = t.matmul(dst,
                                           self.wob2[:, 1024 * kt + 128 * fc:
                                                     1024 * kt + 128 * (fc + 1)],
                                           self.ohU[:, CH * kt:CH * (kt + 1)],
                                           start=(kt == 0), stop=(kt == 3))
                        self.inc(ins, P + "ymm", 1)
                    if not inGI:
                        continue
                    self.w(t, P + "sqv", zi + 1)
                    self.w(t, P + "exS", 32 * (ei + 1))
                    if zi >= 1:
                        self.w(t, P + "lsV", zi)
                    for ft in range(NFT):
                        b = 528 * ft + (16 if ft < 4 else 15)
                        ins = t.matmul(psA[0:1, 1024:1536], self.onesb[:, 0:1],
                                       Xbf[:, b:b + 512],
                                       start=(ft == 0), stop=(ft == NFT - 1))
                    self.inc(ins, P + "lsT", 1)
                    for ft in range(NFT):
                        b = 528 * ft + (16 if ft < 4 else 15)
                        ins = t.matmul(psA[0:1, 1536:2048], self.onesb[:, 0:1],
                                       self.x2bf[:, b:b + 512],
                                       start=(ft == 0), stop=(ft == NFT - 1))
                    self.inc(ins, P + "lsT", 1)
                    self.w(t, P + "lsV2", zi + 1)
                    self.w(t, P + "xmidv", NFT * (ei + 1))
                    t.matmul(psA[:, 0:512], self.onesrow[0:1, 0:128],
                             self.yrows[0:1, 4 * CH:5 * CH], start=True, stop=True)
                    ins = t.matmul(psA[:, 512:1024], self.onesrow[0:1, 0:128],
                                   self.yrows[0:1, CH:2 * CH], start=True, stop=True)
                    self.inc(ins, P + "l2pT", 1)
                    self.w(t, P + "xt2v", NFT * (zi + 1))
                    for j in range(22):
                        gj = 22 * zi + j
                        wj = 128 if j < 21 else 42
                        self.w(t, P + f"w1ld{j % 2}",
                               16 * (11 * zi + j // 2 + 1))
                        if gj >= 2:
                            self.w(t, P + "hv", gj - 1)
                        bA = 1024 * (j % 2)
                        for kt in range(NFT):
                            t.matmul(psA[0:wj, bA:bA + 512],
                                     self.w1s[j % 2][:, 2 * wj * kt:
                                                    2 * wj * kt + wj],
                                     self.xt2[:, CH * kt:CH * kt + 512],
                                     start=(kt == 0), stop=(kt == NFT - 1))
                        for kt in range(NFT):
                            ins = t.matmul(psA[0:wj, bA + 512:bA + 1024],
                                           self.w1s[j % 2][:, 2 * wj * kt + wj:
                                                          2 * wj * kt + 2 * wj],
                                           self.xt2[:, CH * kt:CH * kt + 512],
                                           start=(kt == 0), stop=(kt == NFT - 1))
                        self.inc(ins, P + "hT", 1)
                    self.w(t, P + "hv", 22 * (zi + 1))
                    for kt in range(22):
                        r = 128 if kt < 21 else 42
                        self.w(t, P + f"w2ld{kt % 2}",
                               16 * (11 * zi + kt // 2 + 1))
                        for fc in range(NFT):
                            ps = psA if fc < 4 else psB
                            dst = ps[:, 512 * (fc % 4):512 * (fc % 4) + 512]
                            ins = t.matmul(dst,
                                           self.w2s[kt % 2][0:r, 128 * fc:
                                                           128 * (fc + 1)],
                                           self.hbuf[0:r, 512 * kt:512 * (kt + 1)],
                                           start=(kt == 0), stop=(kt == 21))
                        self.inc(ins, P + "iT", 1)

            @blk.scalar
            def _(sc):
                for c in BCH:
                    bi, inEF, ei, inGI, zi, nef, ngi, bp, nefp = idx(c)
                    self.w(sc, P + f"ild{c % 2}", 16 * (bp + 1))
                    self.w(sc, P + "sqv", ngi)
                    self.w(sc, P + "ocmm", 12 * bi)
                    self.w(sc, P + "lsT", 2 * ngi)
                    self.w(sc, P + "xt2v", NFT * ngi)
                    self.w(sc, P + "hsv", nef)
                    self.w(sc, P + "hsa", bi)
                    for ft in range(NFT):
                        if l == 0:
                            self.w(sc, P + "apv", NFT * bi + ft + 1)
                        ins = sc.activation(
                            Xbf[:, 528 * ft + 16:528 * ft + 528],
                            self.xs[c % 2][:, CH * ft:CH * (ft + 1)], AF.Copy)
                        self.inc(ins, P + "xbv", 1)
                    for oc in range(8):
                        qki = 8 * bi + oc
                        self.w(sc, P + "ocmm", 12 * bi + oc + 1)
                        self.inc(sc.activation(self.q2b[:, :],
                                               psA[:, 512 * (oc % 2):
                                                   512 * (oc % 2) + 512],
                                               AF.Square), P + "sqS", 1)
                        self.w(sc, P + "ssT", qki + 1)
                        self.inc(sc.activation(self.sd8[:, :],
                                               psB[:, 8 * oc:8 * oc + 8],
                                               AF.Sqrt), P + "l2sS", 1)
                    if not inEF:
                        continue
                    kks = list(range(4 * c - 4, 4 * c + 4))
                    NKK = 8
                    u0c = 32 * ei
                    for hp in range(4):
                        for i, kk in enumerate(kks):
                            u = u0c + hp * NKK + i
                            d = kk - 4 * c
                            qlo = 128 * max(0, d)
                            qhi = 128 * (min(3, d + 4) + 1)
                            self.w(sc, P + "sT", u + 1)
                            if u >= 4:
                                self.w(sc, P + "pvT", u - 3)
                            bS = 1024 * (u % 2)
                            bL = 1024 * (u % 4)
                            if qlo == 0 and qhi == 512:
                                ins = sc.activation(self.pbL[:, bL:bL + 1024],
                                                    psA[:, bS:bS + 1024],
                                                    AF.Exp)
                            else:
                                for h in range(2):
                                    ins = sc.activation(
                                        self.pbL[:, bL + 512 * h + qlo:
                                                 bL + 512 * h + qhi],
                                        psA[:, bS + 512 * h + qlo:
                                            bS + 512 * h + qhi],
                                        AF.Exp)
                            self.inc(ins, P + "exS", 1)
                    if self.dbg and l == 0:
                        self.w(sc, P + "normv", 4 * (ei + 1))
                        self.inc(sc.dma_start(
                            out=self.dram.get("dbgOH")[:, 2048 * c:2048 * (c + 1)],
                            in_=self.ohU[:, :]), "dbgOHw", 16)
                    self.w(sc, P + "ocmm", 12 * (bi + 1))
                    self.w(sc, P + "hsa", bi + 1)
                    for fc in range(NFT):
                        self.w(sc, P + "xmidv", NFT * ei + fc + 1)
                        self.inc(sc.activation(
                            Xbf[:, 528 * fc + 16:528 * fc + 528],
                            self.xs[c % 2][:, CH * fc:CH * (fc + 1)], AF.Copy),
                            P + "xmbf", 1)
                    if not inGI:
                        continue
                    self.w(sc, P + "xmbf", NFT * (ei + 1))
                    self.w(sc, P + "hfrv", ei + 1)
                    for ft in range(NFT):
                        b = 528 * ft + 15
                        ins = sc.activation(self.x2bf[:, b:b + 513],
                                            Xbf[:, b:b + 513], AF.Square)
                    self.inc(ins, P + "sqv", 1)
                    self.w(sc, P + "lsV1", zi + 1)
                    self.inc(sc.activation(self.yrows[0:1, 3 * CH:4 * CH],
                                           self.yrows[0:1, 2 * CH:3 * CH],
                                           AF.Sqrt, bias=self.epscol[0:1, 0:1]),
                             P + "lsS", 1)
                    for j in range(22):
                        gj = 22 * zi + j
                        wj = 128 if j < 21 else 42
                        bA = 1024 * (j % 2)
                        self.w(sc, P + "hT", gj + 1)
                        if gj >= 1:
                            self.w(sc, P + "hv", gj)
                        self.inc(sc.activation(self.gelub[0:wj, 0:512],
                                               psA[0:wj, bA + 512:bA + 1024],
                                               AF.Gelu,
                                               bias=self.b1gcol[0:wj, j:j + 1]),
                                 P + "geS", 1)

            @blk.vector
            def _(v):
                for c in BCH:
                    bi, inEF, ei, inGI, zi, nef, ngi, bp, nefp = idx(c)
                    if l == 0 and c == BCH[0]:
                        self.w(v, "cld")
                    self.w(v, P + "tld0" if c % 2 == 0 else P + "tld1", 64 * (bp + 1))
                    v.tensor_scalar(self.tabs[c % 2][:, 0:CH],
                                    self.tabs[c % 2][:, 0:CH],
                                    self.qscol[:, 0:1], 8.0, OP.mult, OP.mult)
                    v.tensor_scalar(self.tabs[c % 2][:, CH:2 * CH],
                                    self.tabs[c % 2][:, CH:2 * CH],
                                    self.qsswcol[:, 0:1], 8.0, OP.mult, OP.mult)
                    v.tensor_scalar(self.tabs[c % 2][:, 2 * CH:3 * CH],
                                    self.tabs[c % 2][:, 2 * CH:3 * CH],
                                    self.kscol[:, 0:1], None, OP.mult)
                    self.inc(v.tensor_scalar(self.tabs[c % 2][:, 3 * CH:4 * CH],
                                             self.tabs[c % 2][:, 3 * CH:4 * CH],
                                             self.ksswcol[:, 0:1], None, OP.mult),
                             P + "tabv", 1)
                    self.w(v, P + f"ild{c % 2}", 16 * (bp + 1))
                    self.w(v, P + "sqv", ngi)
                    self.w(v, P + "ocmm", 12 * bi)
                    if self.dbg and l == 0 and nef >= 1:
                        self.w(v, "dbgMw", 16 * nef)
                    if c == BCH[0]:
                        ins = v.memset(Xbf.ap()[:, 15:528 * NFT:528], 0.0)
                    else:
                        ins = v.tensor_copy(Xbf.ap()[:, 15:528 * NFT:528],
                                            self.halocols[:, 0:NFT])
                    self.inc(ins, P + "harv", 1)
                    self.w(v, P + "harv", bi + 1)
                    if l == 0:
                        self.w(v, P + "aplT", 2 * (bi + 1))
                        for ft in range(NFT):
                            self.inc(v.tensor_tensor(
                                self.plane[:, :],
                                self.xs[c % 2][:, CH * ft:CH * (ft + 1)],
                                psA[:, 1024:1536], OP.mult), "vch", 1)
                            self.w(v, "vch")
                            self.inc(v.tensor_add(
                                self.xs[c % 2][:, CH * ft:CH * (ft + 1)],
                                self.plane[:, :], psA[:, 1536:2048]),
                                P + "apv", 1)
                            self.w(v, P + "apv", NFT * bi + ft + 1)
                    self.w(v, P + "xbv", NFT * (bi + 1))
                    self.inc(v.tensor_copy(self.halocols[:, 0:NFT],
                                           Xbf.ap()[:, 527:528 * NFT:528]),
                             P + "hsa", 1)
                    for oc in range(12):
                        gi = 12 * bi + oc
                        self.w(v, P + "ocmm", gi + 1)
                        if oc < 8:
                            qki = 8 * bi + oc
                            self.w(v, P + "l2sS", qki + 1)
                            self.inc(v.tensor_scalar(self.rc8[:, :], self.sd8[:, :],
                                                     NEG12, None, OP.max), "vch", 1)
                            self.w(v, "vch")
                            self.inc(v.reciprocal(self.rc8[:, :], self.rc8[:, :]),
                                     P + "ssV", 1)
                            self.w(v, P + "ssV")
                            self.inc(v.tensor_copy(self.rc8b[:, :], self.rc8[:, :]),
                                     P + "ssV2", 1)
                            self.w(v, P + "rqT", qki + 1)
                            self.inc(v.tensor_copy(self.rqrows[0:2, 0:CH],
                                                   psBb[0:2, 1024:1536]),
                                     P + "rqE", 1)
                            self.inc(v.engine_nop(), P + "rqE2", 1)
                            self.w(v, P + "plT", qki + 1)
                            self.inc(v.tensor_copy(self.plane[:, :],
                                                   psA[:, 1536:2048]), "vch", 1)
                            self.w(v, "vch")
                            if qki >= 1:
                                self.w(v, P + "swT", qki)
                            self.inc(v.tensor_tensor(
                                self.t1b[:, :],
                                psA[:, 512 * (oc % 2):512 * (oc % 2) + 512],
                                self.plane[:, :], OP.mult), P + "t1v", 1)
                            self.w(v, P + "swT", qki + 1)
                            tb = 1 if oc < 4 else 3
                            self.inc(v.tensor_tensor(
                                self.bb[:, :], psA[:, 1024:1536],
                                self.tabs[c % 2][:, CH * tb:CH * (tb + 1)],
                                OP.mult), P + "bv", 1)
                            self.inc(v.tensor_tensor(
                                self.ab[:, :], self.t1b[:, :],
                                self.tabs[c % 2][:, CH * (0 if oc < 4 else 2):
                                                 CH * (1 if oc < 4 else 3)],
                                OP.mult), "vch", 1)
                            self.w(v, "vch")
                            if oc == 0:
                                self.w(v, P + "sT", 32 * nef)
                            hp = oc % 4
                            if oc < 4:
                                dst = self.qhat[:, hp * CH:(hp + 1) * CH]
                            else:
                                dst = self.khat[:, (c % 2) * 4 * CH + hp * CH:
                                                (c % 2) * 4 * CH + (hp + 1) * CH]
                            self.inc(v.tensor_tensor(dst, self.ab[:, :],
                                                     self.bb[:, :], OP.add),
                                     P + "qkv2", 1)
                            self.inc(v.engine_nop(), P + "ocv", 1)
                        else:
                            hpv = oc - 8
                            vi = 4 * bi + hpv
                            if hpv == 0:
                                self.w(v, P + "pvT", 32 * nef)
                            self.inc(v.tensor_copy(
                                self.vraw[:, :],
                                psA[:, 512 * (oc % 2):512 * (oc % 2) + 512]),
                                P + "vbE", 1)
                            self.w(v, P + "vtT", vi + 1)
                            for ts in range(4):
                                base = (c % 2) * 4 * 528 + 528 * ts + 132 * hpv
                                ins = v.tensor_copy(
                                    self.vtm.ap()[:, base:base + 132]
                                    .rearrange("p (h x) -> p h x", h=2)[:, :, 0:64],
                                    psBb[:, 2048 + 128 * ts:2048 + 128 * (ts + 1)]
                                    .rearrange("p (h x) -> p h x", h=2))
                            self.inc(ins, P + "vtE", 1)
                            self.inc(v.engine_nop(), P + "ocv", 1)
                    if not inEF:
                        continue
                    kks = list(range(4 * c - 4, 4 * c + 4))
                    NKK = 8
                    u0c = 32 * ei
                    for hp in range(4):
                        for i, kk in enumerate(kks):
                            u = u0c + hp * NKK + i
                            d = kk - 4 * c
                            qlo = 128 * max(0, d)
                            qhi = 128 * (min(3, d + 4) + 1)
                            if u >= 4:
                                self.w(v, P + "pvT", u - 3)
                            for h in range(2):
                                base = 1024 * (u % 4) + 512 * h
                                if qlo > 0:
                                    self.inc(v.memset(
                                        self.pbL[:, base:base + qlo], 0.0),
                                        "vch", 1)
                                if qhi < 512:
                                    self.inc(v.memset(
                                        self.pbL[:, base + qhi:base + 512], 0.0),
                                        "vch", 1)
                            self.w(v, P + "exS", u + 1)
                            q = 128 * d if d >= 0 else 128 * (d + 4)
                            m = self.trim[:, 0:128] if d >= 0 \
                                else self.trim[:, 128:256]
                            for h in range(2):
                                base = 1024 * (u % 4) + 512 * h
                                ins = v.tensor_tensor(
                                    self.pbL[:, base + q:base + q + 128],
                                    self.pbL[:, base + q:base + q + 128],
                                    m, OP.mult)
                            if kk < 12:
                                self.inc(ins, "vch", 1)
                                self.w(v, "vch")
                                for h in range(2):
                                    base = 1024 * (u % 4) + 512 * h
                                    ins = v.tensor_scalar(
                                        self.pbL[:, base:base + 512],
                                        self.pbL[:, base:base + 512],
                                        self.kvc[:, kk:kk + 1], None, OP.mult)
                            self.inc(ins, P + "mkv", 1)
                        self.w(v, P + "pvT", u0c + (hp + 1) * NKK)
                        self.w(v, P + "ymm", NFT * ei)
                        if self.dbg and l == 0 and ei >= 1 and hp == 0:
                            self.w(v, "dbgOHw", 16 * ei)
                        v.tensor_copy(self.ohU[0:64, hp * CH:(hp + 1) * CH],
                                      psB[0:64, 0:512])
                        self.inc(v.tensor_copy(
                            self.ohU[64:128, hp * CH:(hp + 1) * CH],
                            psB[0:64, 512:1024]), P + "odV", 1)
                        self.inc(v.tensor_scalar(self.drcpb[0:1, 0:CH],
                                                 psB[64:65, 0:512],
                                                 1e-30, None, OP.max), "vch", 1)
                        self.inc(v.tensor_scalar(self.drcpb[0:1, CH:2 * CH],
                                                 psB[64:65, 512:1024],
                                                 1e-30, None, OP.max),
                                 P + "drV", 1)
                        self.w(v, P + "dpT", 4 * ei + hp + 1)
                        self.inc(v.reciprocal(psB[:, 1024:1536],
                                              psB[:, 1536:2048]), "vch", 1)
                        self.w(v, "vch")
                        self.inc(v.tensor_tensor(
                            self.ohU[:, hp * CH:(hp + 1) * CH],
                            self.ohU[:, hp * CH:(hp + 1) * CH],
                            psB[:, 1024:1536], OP.mult), P + "normv", 1)
                    if c == EFC[0]:
                        ins = v.memset(Xbf.ap()[:, 15:528 * NFT:528], 0.0)
                    else:
                        ins = v.tensor_copy(Xbf.ap()[:, 15:528 * NFT:528],
                                            self.halocolsF[:, 0:NFT])
                    self.inc(ins, P + "hfrv", 1)
                    for fc in range(NFT):
                        self.w(v, P + "ymm", NFT * ei + fc + 1)
                        self.inc(v.tensor_add(
                            self.xs[c % 2][:, CH * fc:CH * (fc + 1)],
                            self.xs[c % 2][:, CH * fc:CH * (fc + 1)],
                            psA[:, 512 * (fc % 2):512 * (fc % 2) + 512]),
                            P + "xmidv", 1)
                        self.w(v, P + "xmidv")
                    self.w(v, P + "xmbf", NFT * (ei + 1))
                    self.inc(v.tensor_copy(self.halocolsF[:, 0:NFT],
                                           Xbf.ap()[:, 527:528 * NFT:528]),
                             P + "hsv", 1)
                    if not inGI:
                        continue
                    self.w(v, P + "lsT", 2 * (zi + 1))
                    self.inc(v.tensor_scalar(self.yrows[0:1, 0:CH],
                                             psA[0:1, 1024:1536], 1.0 / DM,
                                             None, OP.mult), "vch", 1)
                    self.w(v, "vch")
                    self.inc(v.tensor_tensor(self.yrows[0:1, CH:2 * CH],
                                             self.yrows[0:1, 0:CH],
                                             self.yrows[0:1, 0:CH], OP.mult),
                             "vch", 1)
                    self.w(v, "vch")
                    self.inc(v.scalar_tensor_tensor(
                        self.yrows[0:1, 2 * CH:3 * CH], psA[0:1, 1536:2048],
                        1.0 / DM, self.yrows[0:1, CH:2 * CH],
                        OP.mult, OP.subtract), P + "lsV1", 1)
                    self.w(v, P + "lsS", zi + 1)
                    self.inc(v.reciprocal(self.yrows[0:1, 4 * CH:5 * CH],
                                          self.yrows[0:1, 3 * CH:4 * CH]), "vch", 1)
                    self.w(v, "vch")
                    self.inc(v.tensor_tensor(self.yrows[0:1, 5 * CH:6 * CH],
                                             self.yrows[0:1, 0:CH],
                                             self.yrows[0:1, 4 * CH:5 * CH],
                                             OP.mult), "vch", 1)
                    self.w(v, "vch")
                    self.inc(v.tensor_scalar(self.yrows[0:1, CH:2 * CH],
                                             self.yrows[0:1, 5 * CH:6 * CH],
                                             -1.0, None, OP.mult), P + "lsV2", 1)
                    self.inc(v.engine_nop(), P + "lsV", 1)
                    self.w(v, P + "l2pT", zi + 1)
                    for ft in range(NFT):
                        bsh = 528 * ft + (16 if ft < 4 else 15)
                        self.inc(v.scalar_tensor_tensor(
                            self.xt2[:, CH * ft:CH * (ft + 1)],
                            Xbf[:, bsh:bsh + 512],
                            self.gcol[:, ft:ft + 1], psA[:, 0:512],
                            OP.mult, OP.mult), "vch", 1)
                        self.w(v, "vch")
                        self.inc(v.scalar_tensor_tensor(
                            self.xt2[:, CH * ft:CH * (ft + 1)],
                            psA[:, 512:1024], self.gcol[:, ft:ft + 1],
                            self.xt2[:, CH * ft:CH * (ft + 1)],
                            OP.mult, OP.add), "vch", 1)
                        self.w(v, "vch")
                        self.inc(v.tensor_scalar(
                            self.xt2[:, CH * ft:CH * (ft + 1)],
                            self.xt2[:, CH * ft:CH * (ft + 1)],
                            self.bcol[:, ft:ft + 1], None, OP.add),
                            P + "xt2v", 1)
                    for j in range(22):
                        gj = 22 * zi + j
                        wj = 128 if j < 21 else 42
                        bA = 1024 * (j % 2)
                        self.w(v, P + "geS", gj + 1)
                        self.inc(v.scalar_tensor_tensor(
                            self.hbuf[0:wj, 512 * j:512 * (j + 1)],
                            psA[0:wj, bA:bA + 512],
                            self.b1acol[0:wj, j:j + 1],
                            self.gelub[0:wj, 0:512], OP.add, OP.mult),
                            P + "hv", 1)
                    self.w(v, P + "iT", 22 * (zi + 1))
                    for fc in range(NFT):
                        ps = psA if fc < 4 else psB
                        self.inc(v.scalar_tensor_tensor(
                            self.xs[c % 2][:, CH * fc:CH * fc + 512],
                            ps[:, 512 * (fc % 4):512 * (fc % 4) + 512],
                            self.b2col[:, fc:fc + 1],
                            self.xs[c % 2][:, CH * fc:CH * fc + 512],
                            OP.add, OP.add), P + "outv", 1)


    def phase_final(self):
        nc = self.nc
        dram = self.dram
        psA = self.psA
        P = "F_"
        FCS = list(range(3, NCH_E))
        st3 = contextlib.ExitStack()
        xf1 = self.sbuf("xf1", [128, 2 * NFT * CH], st=st3)
        yt = self.sbuf("yt", [128, CH], st=st3)
        tt = self.sbuf("tt", [128, CH], st=st3)
        with st3:
            with nc.Block() as blk:
                @blk.sync
                def _(sy):
                    for fi, c in enumerate(FCS):
                        if fi >= 2:
                            self.w(sy, P + "yv", NFT * (fi - 1))
                        self.inc(sy.dma_start(
                            out=self.xs[c % 2].ap().rearrange(
                                "p (a c) -> p a c", a=NFT),
                            in_=dram["st_in"][:, CH * c:CH * (c + 1)].rearrange(
                                "(a p) c -> p a c", p=128)),
                            P + f"ild{c % 2}", 16)
                        if fi >= 2:
                            self.w(sy, P + "yv2", NFT * (fi - 1))
                        off = (c % 2) * NFT * CH
                        self.inc(sy.dma_start(
                            out=xf1.ap()[:, off:off + NFT * CH].rearrange(
                                "p (a c) -> p a c", a=NFT),
                            in_=dram["st_x1"][:, CH * c:CH * (c + 1)].rearrange(
                                "(a p) c -> p a c", p=128)),
                            P + f"xld{c % 2}", 16)

                @blk.vector
                def _(v):
                    for fi, c in enumerate(FCS):
                        self.w(v, P + f"ild{c % 2}", 16 * (fi // 2 + 1))
                        self.w(v, P + "sq", fi)
                        for ft in range(NFT):
                            ins = v.tensor_copy(self.xbf[:, CH * ft:CH * (ft + 1)],
                                                self.xs[c % 2][:, CH * ft:CH * (ft + 1)])
                        self.inc(ins, P + "cv", 1)
                        self.w(v, P + "sT", 2 * (fi + 1))
                        self.inc(v.tensor_scalar(self.yrows[0:1, 0:CH],
                                                 psA[0:1, 1024:1536], 1.0 / DM,
                                                 None, OP.mult), "vch", 1)
                        self.w(v, "vch")
                        self.inc(v.tensor_tensor(self.yrows[0:1, CH:2 * CH],
                                                 self.yrows[0:1, 0:CH],
                                                 self.yrows[0:1, 0:CH], OP.mult),
                                 "vch", 1)
                        self.w(v, "vch")
                        self.inc(v.scalar_tensor_tensor(
                            self.yrows[0:1, 2 * CH:3 * CH], psA[0:1, 1536:2048],
                            1.0 / DM, self.yrows[0:1, CH:2 * CH],
                            OP.mult, OP.subtract), P + "v1", 1)
                        self.w(v, P + "sd", fi + 1)
                        self.inc(v.reciprocal(self.yrows[0:1, 4 * CH:5 * CH],
                                              self.yrows[0:1, 3 * CH:4 * CH]),
                                 "vch", 1)
                        self.w(v, "vch")
                        self.inc(v.tensor_tensor(self.yrows[0:1, 5 * CH:6 * CH],
                                                 self.yrows[0:1, 0:CH],
                                                 self.yrows[0:1, 4 * CH:5 * CH],
                                                 OP.mult), "vch", 1)
                        self.w(v, "vch")
                        self.inc(v.tensor_scalar(self.yrows[0:1, CH:2 * CH],
                                                 self.yrows[0:1, 5 * CH:6 * CH],
                                                 -1.0, None, OP.mult), P + "v2", 1)
                        self.w(v, P + "pT", fi + 1)
                        self.w(v, P + f"xld{c % 2}", 16 * (fi // 2 + 1))
                        for ft in range(NFT):
                            if fi >= 1 or ft >= 1:
                                self.w(v, P + "yv3", NFT * fi + ft)
                            self.inc(v.tensor_tensor(
                                tt[:, :], self.xbf[:, CH * ft:CH * (ft + 1)],
                                psA[:, 0:512], OP.mult), "vch", 1)
                            self.w(v, "vch")
                            self.inc(v.tensor_add(tt[:, :], tt[:, :],
                                                  psA[:, 512:1024]), "vch", 1)
                            self.w(v, "vch")
                            if fi >= 1 or ft >= 1:
                                self.w(v, P + "yw", 16 * (NFT * fi + ft - 1) + 16)
                            off = (c % 2) * NFT * CH
                            self.inc(v.tensor_add(yt[:, :], tt[:, :],
                                                  xf1[:, off + CH * ft:off + CH * (ft + 1)]),
                                     P + "yv3", 1)
                            self.inc(v.engine_nop(), P + "yv", 1)
                            self.inc(v.engine_nop(), P + "yv2", 1)

                @blk.gpsimd
                def _(g):
                    for fi, c in enumerate(FCS):
                        for ft in range(NFT):
                            self.w(g, P + "yv3", NFT * fi + ft + 1)
                            self.inc(g.dma_start(
                                out=dram["y"][128 * ft:128 * (ft + 1),
                                              CH * c:CH * (c + 1)],
                                in_=yt[:, :]), P + "yw", 16)
                        if fi == len(FCS) - 1:
                            self.w(g, P + "yw", 16 * NFT * len(FCS))

                @blk.tensor
                def _(t):
                    for fi, c in enumerate(FCS):
                        self.w(t, P + "sq", fi + 1)
                        if fi >= 1:
                            self.w(t, P + "v1", fi)
                        for ft in range(NFT):
                            ins = t.matmul(psA[0:1, 1024:1536], self.onesb[:, 0:1],
                                           self.xbf[:, CH * ft:CH * (ft + 1)],
                                           start=(ft == 0), stop=(ft == NFT - 1))
                        self.inc(ins, P + "sT", 1)
                        for ft in range(NFT):
                            ins = t.matmul(psA[0:1, 1536:2048], self.onesb[:, 0:1],
                                           self.x2bf[:, CH * ft:CH * (ft + 1)],
                                           start=(ft == 0), stop=(ft == NFT - 1))
                        self.inc(ins, P + "sT", 1)
                        self.w(t, P + "v2", fi + 1)
                        if fi >= 1:
                            self.w(t, P + "yv3", NFT * fi)
                        t.matmul(psA[:, 0:512], self.onesrow[0:1, 0:128],
                                 self.yrows[0:1, 4 * CH:5 * CH],
                                 start=True, stop=True)
                        ins = t.matmul(psA[:, 512:1024], self.onesrow[0:1, 0:128],
                                       self.yrows[0:1, CH:2 * CH],
                                       start=True, stop=True)
                        self.inc(ins, P + "pT", 1)

                @blk.scalar
                def _(sc):
                    for fi, c in enumerate(FCS):
                        self.w(sc, P + "cv", fi + 1)
                        for ft in range(NFT):
                            b = CH * ft
                            ins = sc.activation(self.x2bf[:, b:b + 512],
                                                self.xbf[:, b:b + 512],
                                                AF.Square)
                        self.inc(ins, P + "sq", 1)
                        self.w(sc, P + "v1", fi + 1)
                        self.inc(sc.activation(self.yrows[0:1, 3 * CH:4 * CH],
                                               self.yrows[0:1, 2 * CH:3 * CH],
                                               AF.Sqrt, bias=self.epscol[0:1, 0:1]),
                                 P + "sd", 1)

    def finish(self):
        nc = self.nc
        with nc.Block() as blk:
            @blk.vector
            def _(v):
                self.inc(v.memset(self.plane[:, :], 0.0), "fin_v", 1)

            @blk.gpsimd
            def _(g):
                self.w(g, "fin_v", 1)
                for fc in range(NFT):
                    for i in range(NCH_E):
                        self.inc(g.dma_start(
                            out=self.dram["y"][128 * fc:128 * (fc + 1), CH * i:CH * (i + 1)],
                            in_=self.plane[:, :]), "youtst", 16)
                self.w(g, "youtst")

    def build(self):
        self.declare()
        with self.st:
            self.phase_init()
            if self.phases >= 1:
                self.phase_top_kvq()
            if self.phases >= 2:
                self.phase_top_attn()
            if self.phases >= 3:
                self.phase_top_merge()
            if self.dbg and self.phases == 3:
                self._dbg_dump_merge()
            self.st1.close()
            if self.phases >= 4:
                self.phase_boundary()
                self.declare_inner()
                with self.st2:
                    for l in range(min(2, self.phases - 3)):
                        self.phase_wprep(l)
                        self.phase_inner(l)
                        self.phase_boundary()
            if self.phases >= 6:
                self.phase_final()
            else:
                self.finish()
        return self.nc


def make_tables(e0):
    t = np.arange(e0, e0 + EXT, dtype=np.float64)
    half = np.arange(0, DH, 2, dtype=np.float64)
    inv_freq = 1.0 / 10000.0 ** (half / DH)
    freqs = np.concatenate([t[:, None] * inv_freq[None, :]] * 2, 1)
    base = (half + 0.4 * DH) / (1.4 * DH)
    power = (t - SEQ // 2) / float(WIN // 2)
    scale = np.concatenate([base[None, :] ** power[:, None]] * 2, 1)
    cos, sin = np.cos(freqs), np.sin(freqs)
    sgn = np.where(np.arange(DH) < 32, -1.0, 1.0)[None, :]
    mk = lambda a: np.ascontiguousarray(np.tile(a.T, (2, 1))).astype(BF)
    return (mk(cos * scale), mk(sin * scale * sgn),
            mk(cos / scale), mk(sin / scale * sgn))


def make_in_maps(inputs):
    x = np.asarray(inputs["L0_attn_x"], np.float32)
    maps = []
    shared = {
        "wq": np.asarray(inputs["L0_attn_w_q"], np.float32),
        "wk": np.asarray(inputs["L0_attn_w_k"], np.float32),
        "wv": np.asarray(inputs["L0_attn_w_v"], np.float32),
        "wo_top": np.asarray(inputs["L0_attn_w_o"], np.float32),
        "mask_top": np.ascontiguousarray(
            np.asarray(inputs["L0_attn_head_dim_mask"], np.int32).reshape(DTOP, 1)),
        "identb": np.eye(128, dtype=BF),
        "identf": np.eye(128, dtype=np.float32),
        "psw": np.eye(128, dtype=BF)[np.arange(128) ^ 32],
        "trimasks": np.concatenate(
            [(np.arange(128)[:, None] <= np.arange(128)[None, :]),
             (np.arange(128)[:, None] >= np.arange(128)[None, :])],
            1).astype(BF),
        "bheadb_c": np.kron(np.eye(2), np.ones((1, 64))).astype(BF),
    }
    for l in range(2):
        shared[f"wqkv{l}"] = np.ascontiguousarray(np.asarray(inputs["L0_ffn_wqkv"], np.float32)[l])
        shared[f"wo{l}"] = np.ascontiguousarray(np.asarray(inputs["L0_ffn_wo"], np.float32)[l])
        w1l = np.asarray(inputs["L0_ffn_w1"], np.float32)[l]
        shared[f"w1{l}"] = np.ascontiguousarray(np.concatenate(
            [np.concatenate([w1l[:, 128 * j:min(128 * (j + 1), HID)],
                             w1l[:, HID + 128 * j:min(HID + 128 * (j + 1), 2 * HID)]], 1)
             for j in range(22)], 1))
        shared[f"w2{l}"] = np.ascontiguousarray(np.asarray(inputs["L0_ffn_w2"], np.float32)[l])
        shared[f"b1{l}"] = np.ascontiguousarray(np.asarray(inputs["L0_ffn_b1"], np.float32)[l].reshape(1, HID2))
        shared[f"b2{l}"] = np.ascontiguousarray(np.asarray(inputs["L0_ffn_b2"], np.float32)[l].reshape(1, DM))
        shared[f"lng{l}"] = np.ascontiguousarray(np.asarray(inputs["L0_ffn_ln_g"], np.float32)[l].reshape(1, DM))
        shared[f"lnb{l}"] = np.ascontiguousarray(np.asarray(inputs["L0_ffn_ln_b"], np.float32)[l].reshape(1, DM))
        shared[f"qs{l}"] = np.ascontiguousarray(np.asarray(inputs["L0_ffn_q_scale"], np.float32)[l].reshape(1, DH))
        shared[f"ks{l}"] = np.ascontiguousarray(np.asarray(inputs["L0_ffn_k_scale"], np.float32)[l].reshape(1, DH))

    xT = [np.ascontiguousarray(x[b].T) for b in range(NB)]
    e0s = []
    for core in range(8):
        b, q = divmod(core, 4)
        a0 = OWN * q
        e0 = a0 - HALO
        e0s.append(e0)
        csq, ssq, csk, ssk = make_tables(e0)
        m = dict(shared)
        m["xT"] = xT[b]
        if e0 < 0:
            m["xTe"] = np.ascontiguousarray(np.concatenate(
                [np.zeros((DM, -e0), np.float32), xT[b][:, 0:e0 + EXT]], 1))
        else:
            m["xTe"] = np.ascontiguousarray(xT[b][:, e0:e0 + EXT])
        m["csq"], m["ssq"], m["csk"], m["ssk"] = csq, ssq, csk, ssk
        blk0 = np.arange(28) * 128 + e0
        m["kvc"] = np.ascontiguousarray(np.broadcast_to(
            (blk0 >= 0).astype(np.float32)[None, :], (128, 28)))
        tglob = np.arange(EXT) + e0
        m["tvr"] = np.ascontiguousarray(
            (tglob >= 0).astype(np.float32).reshape(NCH_E, CH))
        maps.append(m)
    return maps, e0s


def kernel(**inputs):
    maps, e0s = make_in_maps(inputs)
    b = B(dbg=False)
    nc = b.build()
    res = run_bass_kernel_spmd(nc, maps, core_ids=list(range(8)))
    out = np.empty((NB, SEQ, DM), np.float32)
    for core in range(8):
        bi, q = divmod(core, 4)
        a0 = OWN * q
        yc = res.results[core]["y"]
        out[bi, a0:a0 + OWN, :] = yc[:, HALO:HALO + OWN].T
    return out


# version 18 (final)
# speedup vs baseline: 1.0532x; 1.0532x over previous
import contextlib
import math

import numpy as np
import ml_dtypes

import concourse.bass as bass
import concourse.mybir as mybir
from concourse.bass_utils import run_bass_kernel_spmd

F32 = mybir.dt.float32
BF16 = mybir.dt.bfloat16
I32 = mybir.dt.int32
AF = mybir.ActivationFunctionType
OP = mybir.AluOpType
AX = mybir.AxisListType

DM = 1024
SEQ = 8192
NB = 2
OWN = 2048
HALO = 1536
EXT = OWN + HALO
NFT = DM // 128
NH = 8
DH = 64
WIN = 512
HID2 = 5460
HID = 2730
DTOP = 128
QK_SCALE = 8.0
TOP_SCALE = 1.0 / math.sqrt(128.0)
EPS = 1e-5
CH = 512
NCH_E = EXT // CH
NCH_A = SEQ // CH
NG_E = EXT // 128
NG_A = SEQ // 128
BF = ml_dtypes.bfloat16


class B:

    def __init__(self, dbg=False, phases=99):
        self.nc = bass.Bass("TRN2", target_bir_lowering=False)
        self.st = contextlib.ExitStack()
        self._sems = {}
        self.semst = contextlib.ExitStack()
        self.cnt = {}
        self.epoch = 0
        self.dram = {}
        self.dbg = dbg
        self.phases = phases
        self.dbg_names = []

    def sem(self, name):
        if name not in self._sems:
            self._sems[name] = self.semst.enter_context(
                self.nc.semaphore(f"{name}_{self.epoch}"))
            self.cnt[name] = 0
        return self._sems[name]

    def phase_boundary(self):
        nc = self.nc
        self.epoch += 1
        clear = [(n, h) for n, h in self._sems.items()]
        alleng = [mybir.EngineType.PE, mybir.EngineType.DVE,
                  mybir.EngineType.Activation, mybir.EngineType.Pool,
                  mybir.EngineType.SP]
        with nc.Block() as blk:
            @blk.gpsimd
            def _(g):
                for n, h in clear:
                    if self.cnt[n] > 0:
                        g.wait_ge(h, self.cnt[n])
        nc.all_engine_barrier()
        with nc.Block() as blk:
            @blk.gpsimd
            def _(g):
                nums = sorted(h.num for _, h in clear)
                ranges = []
                for x in nums:
                    if ranges and ranges[-1][1] == x:
                        ranges[-1][1] = x + 1
                    else:
                        ranges.append([x, x + 1])
                for a, bnd in ranges:
                    g.dma_reset(range(a, bnd))
                for n, h in clear:
                    g.sem_clear(h)
        nc.all_engine_barrier()
        self.semst.close()
        self.semst = contextlib.ExitStack()
        self._sems = {}
        self.cnt = {}

    def inc(self, instr, name, amt=1):
        instr.then_inc(self.sem(name), amt)
        self.cnt[name] += amt
        return instr

    def w(self, eng, name, val=None):
        self.sem(name)
        eng.wait_ge(self._sems[name], self.cnt[name] if val is None else val)

    def sbuf(self, name, shape, dt=F32, st=None):
        return (st or self.st).enter_context(
            self.nc.sbuf_tensor("s_" + name, shape, dt))

    def psum(self, name, shape, dt=F32):
        return self.st.enter_context(self.nc.psum_tensor(name, shape, dt))

    def din(self, name, shape, dt=F32):
        self.dram[name] = self.nc.dram_tensor(name, list(shape), dt,
                                              kind="ExternalInput").ap()
        return self.dram[name]

    def dout(self, name, shape, dt=F32):
        self.dram[name] = self.nc.dram_tensor(name, list(shape), dt,
                                              kind="ExternalOutput").ap()
        return self.dram[name]

    def dint(self, name, shape, dt=F32):
        self.dram[name] = self.nc.dram_tensor(name, list(shape), dt,
                                              kind="Internal").ap()
        return self.dram[name]

    def declare(self):
        din = self.din
        din("xT", (DM, SEQ))
        din("xTe", (DM, EXT))
        din("wq", (DM, DTOP)); din("wk", (DM, DTOP)); din("wv", (DM, DTOP))
        din("wo_top", (DTOP, DM))
        din("mask_top", (DTOP, 1), I32)
        for l in range(2):
            din(f"wqkv{l}", (DM, 3 * 512))
            din(f"wo{l}", (512, DM))
            din(f"w1{l}", (DM, HID2))
            din(f"w2{l}", (HID, DM))
            din(f"b1{l}", (1, HID2))
            din(f"b2{l}", (1, DM))
            din(f"lng{l}", (1, DM))
            din(f"lnb{l}", (1, DM))
            din(f"qs{l}", (1, DH))
            din(f"ks{l}", (1, DH))
            din(f"bcols{l}", (128, 68))
            din(f"qkcols{l}", (4, 128))
        din("csq", (128, EXT), BF16); din("ssq", (128, EXT), BF16)
        din("csk", (128, EXT), BF16); din("ssk", (128, EXT), BF16)
        din("identb", (128, 128), BF16)
        din("identf", (128, 128), F32)
        din("psw", (128, 128), BF16)
        din("trimasks", (128, 256), BF16)
        din("kvc", (128, 28))
        din("tvr", (NCH_E, CH))
        din("bheadb_c", (2, 128), BF16)
        self.dout("y", (DM, EXT))
        if self.dbg:
            self.dout("dbgM", (128, 4224 * 7), BF16)
            self.dout("dbgOH", (128, 2048 * 7), BF16)
        self.dint("st_x1", (DM, EXT))
        self.dint("st_in", (DM, EXT))
        self.dint("w1bf", (DM, HID2), BF16)
        self.dint("w2bf", (HID, DM), BF16)

        self.st1 = contextlib.ExitStack()
        s = self.sbuf
        s1 = lambda n, sh, dt=F32: self.sbuf(n, sh, dt, st=self.st1)
        self.identb = s("identb", [128, 128], BF16)
        self.identf = s("identf", [128, 128], F32)
        self.onesb = s("onesb", [128, 128], BF16)
        self.onesrow = s("onesrow", [1, 128])
        self.rcol_e = s("rcol_e", [128, NG_E])
        self.mucol = s("mucol", [128, 4])
        self.m2col = s("m2col", [128, 4])
        self.varcol = s("varcol", [128, 4])
        self.sdcol = s("sdcol", [128, 4])
        self.sxrow_e = s("sxrow_e", [NCH_E, CH])
        self.r_row_e = s("r_row_e", [1, CH])
        self.xs = [s(f"xs{i}", [128, NFT * CH]) for i in range(2)]
        self.xbf = s("xbf", [128, NFT * 528], BF16)
        self.x2bf = s("x2bf", [128, NFT * 528], BF16)
        self.srow = s("srow", [1, 2 * CH])
        self.srow_b = s("srow_b", [1, 2 * CH], BF16)
        self.vraw = s("vraw", [128, CH], BF16)
        self.plane = s("plane", [128, CH])
        self.rows64 = s("rows64", [64, 128])
        self.mask_i = s("mask_i", [DTOP, 1], I32)
        self.mask_f = s("mask_f", [DTOP, 1])
        self.epscol = s("epscol", [128, 1])
        self.r1row = s("r1row", [NCH_E, CH])
        self.nmr1row = s("nmr1row", [NCH_E, CH])
        self.tvrsb = s("tvrsb", [NCH_E, CH])
        self.yrows = s("yrows", [1, 6 * CH])
        self.rstg = s("rstg", [1, CH])
        self.nstg = s("nstg", [1, CH])
        self.sxstg = s("sxstg", [1, CH])
        self.kT_top = s1("kT_top", [128, SEQ], BF16)
        self.vTM_top = s1("vTM_top", [128, NG_A * 132], BF16)
        self.qT_top = s1("qT_top", [128, EXT], BF16)
        self.rcol_a = s1("rcol_a", [128, NG_A])
        self.rtcol_a = s1("rtcol_a", [128, NG_A])
        self.wkvq = s1("wkvq", [128, NFT * 3 * DTOP], BF16)
        self.csumr = s1("csumr", [1, 3 * DTOP])
        self.csumr_b = s1("csumr_b", [1, 3 * DTOP], BF16)
        self.oh_top = s1("oh_top", [128, EXT], BF16)
        self.pb = [s1(f"pb{i}", [128, CH], BF16) for i in range(4)]
        self.rdrowb = s1("rdrowb", [1, CH], BF16)
        self.osb = s1("osb", [128, CH])
        self.ybf = s1("ybf", [128, NFT * CH], BF16)
        self.wo_b = s1("wo_b", [128, DM], BF16)
        self.rwcol = s1("rwcol", [128, 1])
        self.rwcol_b = s1("rwcol_b", [128, 1], BF16)
        self.nmy = s1("nmy", [1, CH])
        self.nmy_b = s1("nmy_b", [1, CH], BF16)
        self.x1s = s1("x1s", [128, NFT * CH])
        self.utile = s1("utile", [128, CH])

        self.psA = self.psum("psA", [128, 2048])
        self.psB = self.psum("psB", [128, 2048])

    def phase_init(self):
        nc = self.nc
        with nc.Block() as blk:
            @blk.sync
            def _(sy):
                self.inc(sy.dma_start(out=self.identb[:, :], in_=self.dram["identb"]), "ld0", 16)
                self.inc(sy.dma_start(out=self.identf[:, :], in_=self.dram["identf"]), "ld0", 16)
                self.inc(sy.dma_start(out=self.mask_i[:, :], in_=self.dram["mask_top"]), "ld0", 16)

            @blk.vector
            def _(v):
                v.memset(self.onesb[:, :], 1.0)
                v.memset(self.onesrow[:, :], 1.0)
                v.memset(self.epscol[:, :], EPS)
                self.w(v, "ld0")
                self.inc(v.tensor_copy(self.mask_f[:, :], self.mask_i[:, :]), "initv", 1)

    def phase_top_kvq(self):
        nc = self.nc
        dram = self.dram
        psA, psB = self.psA, self.psB
        psBb = psB.ap().bitcast(BF16)

        with nc.Block() as blk:
            @blk.gpsimd
            def _(g):
                for j, wn in enumerate(("wq", "wk", "wv")):
                    self.inc(g.dma_start(
                        out=self.wkvq.ap().rearrange("p (a c) -> p a c", a=NFT)
                        [:, :, 128 * j:128 * (j + 1)],
                        in_=dram[wn].rearrange("(a p) c -> p a c", p=128)),
                        "wldg", 16)

            @blk.tensor
            def _(t):
                self.w(t, "wldg", 48)
                for ft in range(NFT):
                    ins = t.matmul(psA[0:1, 0:384], self.onesb[:, 0:1],
                                   self.wkvq[:, ft * 384:(ft + 1) * 384],
                                   start=(ft == 0), stop=(ft == NFT - 1))
                self.inc(ins, "csmm", 1)

            @blk.vector
            def _(v):
                self.w(v, "csmm", 1)
                self.inc(v.tensor_scalar(self.csumr[:, :], psA[0:1, 0:384],
                                         -1.0 / DM, None, OP.mult), "vch", 1)
                self.w(v, "vch")
                self.inc(v.tensor_copy(self.csumr_b[:, :], self.csumr[:, :]),
                         "csv", 1)

        chunks = [("a", i) for i in range(NCH_A)] + [("e", i) for i in range(NCH_E)]

        def proj(t, dst, wslice, corr_sl, n):
            for ft in range(NFT):
                t.matmul(dst, self.wkvq[:, ft * 384 + wslice:ft * 384 + wslice + 128],
                         self.xbf[:, CH * ft:CH * (ft + 1)],
                         start=(ft == 0), stop=False)
            return t.matmul(dst, self.csumr_b[0:1, corr_sl:corr_sl + 128],
                            self.srow_b[0:1, 0:CH], start=False, stop=True)

        with nc.Block() as blk:
            @blk.sync
            def _(sy):
                for n, (kind, i) in enumerate(chunks):
                    src = dram["xT"] if kind == "a" else dram["xTe"]
                    if n >= 2:
                        self.w(sy, "cvt", n - 1)
                    for ft in range(NFT):
                        self.inc(sy.dma_start(
                            out=self.xs[n % 2][:, CH * ft:CH * (ft + 1)],
                            in_=src[128 * ft:128 * (ft + 1), CH * i:CH * (i + 1)]),
                            f"xld{n % 2}", 16)
                    if kind == "e":
                        self.w(sy, "rrE", i + 1)
                        if i >= 1:
                            self.w(sy, "qpl", i)
                        self.inc(sy.dma_start(
                            out=self.r_row_e[0:1, 0:CH],
                            in_=self.rows64[0:4, :]), "rfl", 16)

            @blk.gpsimd
            def _(g):
                for n, (kind, i) in enumerate(chunks):
                    self.w(g, "cvt", n + 1)
                    self.inc(g.tensor_tensor(self.x2bf[:, 0:NFT * CH],
                                             self.xbf[:, 0:NFT * CH],
                                             self.xbf[:, 0:NFT * CH], OP.mult),
                             "sq", 1)
                    if kind == "e":
                        self.w(g, "srowE", n + 1)
                        self.inc(g.dma_start(out=self.sxrow_e[i:i + 1, 0:CH],
                                             in_=self.srow[0:1, 0:CH]), "sxw", 16)

            @blk.vector
            def _(v):
                self.w(v, "csv", 1)
                self.w(v, "initv", 1)
                for n, (kind, i) in enumerate(chunks):
                    self.w(v, f"xld{n % 2}", 16 * NFT * (n // 2 + 1))
                    for ft in range(NFT):
                        ins = v.tensor_copy(self.xbf[:, CH * ft:CH * (ft + 1)],
                                            self.xs[n % 2][:, CH * ft:CH * (ft + 1)])
                    self.inc(ins, "cvt", 1)
                    self.w(v, "smm", 2 * (n + 1))
                    if n >= 1 and chunks[n - 1][0] == "e":
                        self.w(v, "sxw", 16 * chunks[n - 1][1] + 16)
                    self.inc(v.tensor_copy(self.srow[:, :], psA[0:1, 0:1024]), "vch", 1)
                    self.w(v, "vch")
                    v.tensor_copy(self.srow_b[:, :], self.srow[:, :])
                    self.inc(v.engine_nop(), "srowE", 1)
                    self.w(v, "stxT", n + 1)
                    self.inc(v.tensor_scalar(self.mucol[:, :], psB[:, 0:4], 1.0 / DM,
                                             None, OP.mult), "vch", 1)
                    self.w(v, "vch")
                    self.inc(v.tensor_tensor(self.m2col[:, :], self.mucol[:, :],
                                             self.mucol[:, :], OP.mult), "vsm1", 1)
                    self.w(v, "vsm1")
                    self.inc(v.scalar_tensor_tensor(
                        self.varcol[:, :], psB[:, 4:8], 1.0 / DM,
                        self.m2col[:, :], OP.mult, OP.subtract), "vsm2", 1)
                    self.w(v, "ssm", n + 1)
                    rcol = self.rcol_a if kind == "a" else self.rcol_e
                    self.inc(v.reciprocal(rcol[:, 4 * i:4 * i + 4],
                                          self.sdcol[:, :]), "vsm3", 1)
                    self.w(v, "vsm3")
                    if kind == "a":
                        self.inc(v.tensor_scalar(
                            self.rtcol_a[:, 4 * i:4 * i + 4],
                            rcol[:, 4 * i:4 * i + 4], TOP_SCALE, None, OP.mult),
                            "smalls", 1)
                    else:
                        self.inc(v.engine_nop(), "smalls", 1)
                    self.w(v, "pmm", n + 1)
                    if kind == "a":
                        self.inc(v.tensor_copy(
                            self.kT_top[:, CH * i:CH * (i + 1)], psA[:, 1024:1536]),
                            "kevac", 1)
                        self.inc(v.tensor_copy(self.vraw[:, :], psA[:, 1536:2048]),
                                 "vevac", 1)
                        self.w(v, "vtrT", n + 1)
                        for gx in range(4):
                            tb = 132 * (4 * i + gx)
                            ins = v.tensor_scalar(
                                self.vTM_top.ap()[:, tb:tb + 132].rearrange(
                                    "p (h x) -> p h x", h=2)[:, :, 0:64],
                                psBb[:, 1024 + 128 * gx:1024 + 128 * (gx + 1)]
                                .rearrange("p (h x) -> p h x", h=2),
                                self.rcol_a[:, 4 * i + gx:4 * i + gx + 1],
                                None, OP.mult)
                        self.inc(ins, "vtme", 1)
                    else:
                        self.w(v, "rrT", i + 1)
                        self.w(v, "rfl", 16 * i)
                        self.inc(v.tensor_copy(self.rows64[0:4, :],
                                               psB[:, 512:1024][0:4, 0:128]), "rrE", 1)
                        self.w(v, "qpl", i + 1)
                        self.inc(v.tensor_copy(self.plane[:, :], psB[:, 1536:2048]),
                                 "qple", 1)
                        self.w(v, "qple")
                        self.inc(v.tensor_tensor(
                            self.qT_top[:, CH * i:CH * (i + 1)], psB[:, 1024:1536],
                            self.plane[:, :], OP.mult), "qevac", 1)

            @blk.scalar
            def _(sc):
                for n, (kind, i) in enumerate(chunks):
                    self.w(sc, "vsm2", n + 1)
                    if n >= 1:
                        self.w(sc, "vsm3", n)
                    self.inc(sc.activation(self.sdcol[:, :], self.varcol[:, :],
                                           AF.Sqrt, bias=self.epscol[:, 0:1]), "ssm", 1)

            @blk.tensor
            def _(t):
                for n, (kind, i) in enumerate(chunks):
                    self.w(t, "cvt", n + 1)
                    self.w(t, "sq", n + 1)
                    if n >= 1:
                        self.w(t, "srowE", n)
                    for ft in range(NFT):
                        ins = t.matmul(psA[0:1, 0:512], self.onesb[:, 0:1],
                                       self.xbf[:, CH * ft:CH * (ft + 1)],
                                       start=(ft == 0), stop=(ft == NFT - 1))
                    self.inc(ins, "smm", 1)
                    for ft in range(NFT):
                        ins = t.matmul(psA[0:1, 512:1024], self.onesb[:, 0:1],
                                       self.x2bf[:, CH * ft:CH * (ft + 1)],
                                       start=(ft == 0), stop=(ft == NFT - 1))
                    self.inc(ins, "smm", 1)
                    self.w(t, "srowE", n + 1)
                    if n >= 1:
                        self.w(t, "vsm2", n)
                    for gx in range(4):
                        t.transpose(psB[:, gx:gx + 1],
                                    self.srow[0:1, 128 * gx:128 * (gx + 1)],
                                    self.identf[0:1, 0:1])
                        ins = t.transpose(psB[:, 4 + gx:4 + gx + 1],
                                    self.srow[0:1, 512 + 128 * gx:512 + 128 * (gx + 1)],
                                    self.identf[0:1, 0:1])
                    self.inc(ins, "stxT", 1)
                    if kind == "a":
                        if n >= 1:
                            self.w(t, "kevac", n)
                            self.w(t, "vevac", n)
                        proj(t, psA[:, 1024:1536], 128, 128, n)
                        ins = proj(t, psA[:, 1536:2048], 256, 256, n)
                        self.inc(ins, "pmm", 1)
                        self.w(t, "vevac", n + 1)
                        if n >= 1:
                            self.w(t, "vtme", n)
                        for gx in range(4):
                            ins = t.transpose(psBb[:, 1024 + 128 * gx:1024 + 128 * (gx + 1)],
                                        self.vraw[:, 128 * gx:128 * (gx + 1)],
                                        self.identb[:, :])
                        self.inc(ins, "vtrT", 1)
                    else:
                        if i >= 1:
                            self.w(t, "qevac", i)
                        ins = proj(t, psB[:, 1024:1536], 0, 0, n)
                        self.inc(ins, "pmm", 1)
                        self.w(t, "vsm3", n + 1)
                        if i >= 1:
                            self.w(t, "rrE", i)
                        ins = t.transpose(psB[:, 512:1024][0:4, 0:128],
                                    self.rcol_e[:, 4 * i:4 * i + 4],
                                    self.identf[:, :])
                        self.inc(ins, "rrT", 1)
                        self.w(t, "rfl", 16 * (i + 1))
                        if i >= 1:
                            self.w(t, "qple", i)
                        ins = t.matmul(psB[:, 1536:2048], self.onesrow[0:1, 0:128],
                                 self.r_row_e[0:1, 0:CH],
                                 start=True, stop=True)
                        self.inc(ins, "qpl", 1)

        if self.dbg:
            self._dbg_dump_kvq()

    def _dbg_dump_kvq(self):
        nc = self.nc
        kd = self.dout("dbg_kT", (128, SEQ), BF16)
        vd = self.dout("dbg_vTM", (128, NG_A * 128), BF16)
        qd = self.dout("dbg_qT", (128, EXT), BF16)
        rd = self.dout("dbg_rt", (128, NG_A))
        self.dbg_names += ["dbg_kT", "dbg_vTM", "dbg_qT", "dbg_rt"]
        with nc.Block() as blk:
            @blk.gpsimd
            def _(g):
                self.w(g, "vtme")
                self.w(g, "qevac")
                self.w(g, "smalls")
                self.inc(g.dma_start(out=kd, in_=self.kT_top[:, :]), "dbgst", 16)
                self.inc(g.dma_start(out=vd, in_=self.vTM_top[:, :]), "dbgst", 16)
                self.inc(g.dma_start(out=qd, in_=self.qT_top[:, :]), "dbgst", 16)
                self.inc(g.dma_start(out=rd, in_=self.rtcol_a[:, :]), "dbgst", 16)
                self.w(g, "dbgst")


    def phase_top_attn(self):
        nc = self.nc
        psA, psB = self.psA, self.psB
        NKT = NG_A

        with nc.Block() as blk:
            @blk.tensor
            def _(t):
                self.w(t, "qevac"); self.w(t, "vtme"); self.w(t, "smalls")
                self.w(t, "vtmo", 1)
                SSLOT = (psB.ap()[:, 0:512], psB.ap()[:, 512:1024],
                         psA.ap()[:, 1024:1536], psA.ap()[:, 1536:2048])
                for q4 in range(NCH_E):
                    for kt in range(NKT):
                        g = q4 * NKT + kt
                        psS = SSLOT[kt % 4]
                        if g >= 4:
                            self.w(t, "texp", g - 3)
                        t.matmul(psS, self.kT_top[:, 128 * kt:128 * (kt + 1)],
                                 self.qT_top[:, CH * q4:CH * (q4 + 1)],
                                 start=True, stop=True).then_inc(self.sem("smmT"), 1)
                        self.cnt["smmT"] += 1
                        if kt >= 2:
                            kp = kt - 2
                            self.w(t, "texp", q4 * NKT + kp + 1)
                            if kp == 0 and q4 >= 1:
                                self.w(t, "oev", q4)
                            for h in range(2):
                                ins = t.matmul(
                                    psA[0:65, 512 * h:512 * h + 512],
                                    self.vTM_top[:, 132 * kp + 66 * h:
                                                 132 * kp + 66 * h + 65],
                                    self.pb[kp % 4][:, :],
                                    start=(kp == 0), stop=False)
                            self.inc(ins, "pvT", 1)
                    for kp in (NKT - 2, NKT - 1):
                        self.w(t, "texp", q4 * NKT + kp + 1)
                        for h in range(2):
                            ins = t.matmul(psA[0:65, 512 * h:512 * h + 512],
                                           self.vTM_top[:, 132 * kp + 66 * h:
                                                        132 * kp + 66 * h + 65],
                                           self.pb[kp % 4][:, :],
                                           start=False, stop=(kp == NKT - 1))
                        self.inc(ins, "pvT", 1)
                    self.w(t, "rde", q4 + 1)
                    if q4 >= 1:
                        self.w(t, "ohv", q4)
                    ins = t.matmul(psB[:, 1536:2048], self.onesb[0:1, 0:128],
                                   self.rdrowb[0:1, :], start=True, stop=True)
                    self.inc(ins, "rplT", 1)

            @blk.scalar
            def _(sc):
                SSLOT = (psB.ap()[:, 0:512], psB.ap()[:, 512:1024],
                         psA.ap()[:, 1024:1536], psA.ap()[:, 1536:2048])
                for q4 in range(NCH_E):
                    for kt in range(NKT):
                        g = q4 * NKT + kt
                        self.w(sc, "smmT", g + 1)
                        if g >= 4:
                            self.w(sc, "pvT", g - 3)
                        self.inc(sc.activation(
                            self.pb[kt % 4][:, :], SSLOT[kt % 4],
                            AF.Exp, scale=self.rtcol_a[:, kt:kt + 1]), "texp", 1)

            @blk.vector
            def _(v):
                self.inc(v.memset(self.vTM_top.ap()[:, 64:NG_A * 132:66], 1.0),
                         "vtmo", 1)
                for q4 in range(NCH_E):
                    self.w(v, "pvT", (q4 + 1) * NKT)
                    if q4 >= 1:
                        self.w(v, "ohv", q4)
                    v.tensor_copy(self.osb[0:64, :], psA[0:64, 0:512])
                    v.tensor_copy(self.osb[64:128, :], psA[0:64, 512:1024])
                    self.inc(v.tensor_copy(self.rdrowb[0:1, :],
                                           psA[64:65, 0:512]), "rde", 1)
                    self.inc(v.engine_nop(), "oev", 1)
                    self.w(v, "rplT", q4 + 1)
                    self.inc(v.reciprocal(psB[:, 1024:1536], psB[:, 1536:2048]),
                             "vch", 1)
                    self.w(v, "vch")
                    self.inc(v.tensor_tensor(self.oh_top[:, CH * q4:CH * (q4 + 1)],
                                             self.osb[:, :], psB[:, 1024:1536],
                                             OP.mult), "ohv", 1)

    def phase_top_merge(self):
        nc = self.nc
        dram = self.dram
        psA, psB = self.psA, self.psB

        with nc.Block() as blk:
            @blk.gpsimd
            def _(g):
                self.inc(g.dma_start(out=self.wo_b[:, :],
                                     in_=dram["wo_top"]), "woL", 16)

            @blk.scalar
            def _(sc):
                self.w(sc, "woL", 16)
                self.w(sc, "initv", 1)
                self.inc(sc.activation(self.wo_b[:, :], self.wo_b[:, :],
                                       AF.Copy, scale=self.mask_f[:, 0:1]), "wob", 1)

            @blk.vector
            def _(v):
                self.w(v, "wob", 1)
                self.inc(v.reduce_sum(self.rwcol[:, 0:1], self.wo_b[:, :], AX.X),
                         "vch", 1)
                self.w(v, "vch")
                self.inc(v.tensor_scalar(self.rwcol_b[:, 0:1], self.rwcol[:, 0:1],
                                         -1.0 / DM, None, OP.mult), "rwb", 1)

        base_xld = [self.cnt.get("xld0", 0), self.cnt.get("xld1", 0)]
        with nc.Block() as blk:
            @blk.sync
            def _(sy):
                for c in range(NCH_E):
                    if c >= 2:
                        self.w(sy, "x1v", NFT * (c - 1))
                    for ft in range(NFT):
                        self.inc(sy.dma_start(
                            out=self.xs[c % 2][:, CH * ft:CH * (ft + 1)],
                            in_=dram["xTe"][128 * ft:128 * (ft + 1), CH * c:CH * (c + 1)]),
                            f"xld{c % 2}", 16)

            @blk.tensor
            def _(t):
                self.w(t, "rwb", 1)
                self.w(t, "ohv")
                self.w(t, "texp")
                for c in range(NCH_E):
                    if c >= 1:
                        self.w(t, "syE", c)
                    ins = t.matmul(psA[0:1, 1024:1536], self.rwcol_b[:, 0:1],
                                   self.oh_top[:, CH * c:CH * (c + 1)],
                                   start=True, stop=True)
                    self.inc(ins, "syT", 1)
                    self.w(t, "syE2", c + 1)
                    for fc in range(NFT):
                        if c * NFT + fc >= 2:
                            self.w(t, "yev", c * NFT + fc - 1)
                        psY = psA[:, 512 * (fc % 2):512 * (fc % 2) + 512]
                        t.matmul(psY, self.wo_b[:, 128 * fc:128 * (fc + 1)],
                                 self.oh_top[:, CH * c:CH * (c + 1)],
                                 start=True, stop=False)
                        ins = t.matmul(psY, self.onesb[0:1, 0:128],
                                       self.nmy_b[0:1, :], start=False, stop=True)
                        self.inc(ins, "ymm", 1)
                    self.w(t, "ysq", c + 1)
                    if c >= 1:
                        self.w(t, "ys2E", c)
                    for ft in range(NFT):
                        ins = t.matmul(psB[0:1, 0:512], self.onesb[:, 0:1],
                                       self.x2bf[:, CH * ft:CH * (ft + 1)],
                                       start=(ft == 0), stop=(ft == NFT - 1))
                    self.inc(ins, "ys2T", 1)
                    self.w(t, "yst", c + 1)
                    if c >= 1:
                        self.w(t, "x1v", NFT * c)
                    ins = t.matmul(psB[:, 1024:1536], self.onesrow[0:1, 0:128],
                                   self.yrows[0:1, 4 * CH:5 * CH],
                                   start=True, stop=True)
                    self.inc(ins, "yplT", 1)
                    self.w(t, "x1sq", c + 1)
                    if c >= 1:
                        self.w(t, "x12E", c)
                    for ft in range(NFT):
                        ins = t.matmul(psB[0:1, 512:1024], self.onesb[:, 0:1],
                                       self.x2bf[:, CH * ft:CH * (ft + 1)],
                                       start=(ft == 0), stop=(ft == NFT - 1))
                    self.inc(ins, "x12T", 1)

            @blk.vector
            def _(v):
                for c in range(NCH_E):
                    self.w(v, "syT", c + 1)
                    self.inc(v.tensor_copy(self.nmy[:, :], psA[0:1, 1024:1536]),
                             "vch", 1)
                    self.w(v, "vch")
                    self.inc(v.tensor_copy(self.nmy_b[:, :], self.nmy[:, :]), "syE2", 1)
                    self.inc(v.engine_nop(), "syE", 1)
                    if c >= 1:
                        self.w(v, "ysq", c)
                    for fc in range(NFT):
                        self.w(v, "ymm", c * NFT + fc + 1)
                        self.inc(v.tensor_copy(
                            self.ybf[:, CH * fc:CH * (fc + 1)],
                            psA[:, 512 * (fc % 2):512 * (fc % 2) + 512]), "yev", 1)
                    self.w(v, "ys2T", c + 1)
                    self.inc(v.tensor_scalar(self.yrows[0:1, 0:CH], psB[0:1, 0:512],
                                              1.0 / DM, None, OP.mult), "ys2E", 1)
                    self.w(v, "ysd", c + 1)
                    self.inc(v.reciprocal(self.yrows[0:1, 4 * CH:5 * CH],
                                          self.yrows[0:1, 3 * CH:4 * CH]), "yst", 1)
                    self.w(v, "yplT", c + 1)
                    self.w(v, f"xld{c % 2}", base_xld[c % 2] + 16 * NFT * (c // 2 + 1))
                    if c >= 1:
                        self.w(v, "x1sq", c)
                        self.w(v, "x1w", 16 * NFT * c)
                    for fc in range(NFT):
                        self.inc(v.tensor_tensor(
                            self.utile[:, :],
                            self.ybf[:, CH * fc:CH * (fc + 1)],
                            psB[:, 1024:1536], OP.mult), "vch", 1)
                        self.w(v, "vch")
                        self.inc(v.tensor_add(
                            self.x1s[:, CH * fc:CH * (fc + 1)],
                            self.xs[c % 2][:, CH * fc:CH * (fc + 1)],
                            self.utile[:, :]), "vch", 1)
                        self.w(v, "vch")
                        self.inc(v.tensor_copy(self.xbf[:, CH * fc:CH * (fc + 1)],
                                               self.x1s[:, CH * fc:CH * (fc + 1)]),
                                 "x1v", 1)
                    self.w(v, "x1sq", c + 1)
                    self.w(v, "x12T", c + 1)
                    self.inc(v.tensor_scalar(self.yrows[0:1, 0:CH],
                                             psB[0:1, 512:1024],
                                             1.0 / DM, None, OP.mult), "x12E", 1)
                    self.w(v, "sxr", 16 * (c + 1))
                    self.inc(v.tensor_scalar(self.yrows[0:1, CH:2 * CH],
                                             self.sxstg[0:1, 0:CH],
                                             1.0 / DM, None, OP.mult), "mu1v", 1)
                    self.w(v, "mu1v")
                    self.inc(v.tensor_tensor(self.yrows[0:1, 2 * CH:3 * CH],
                                             self.yrows[0:1, CH:2 * CH],
                                             self.yrows[0:1, CH:2 * CH], OP.mult),
                             "vch", 1)
                    self.w(v, "vch")
                    self.inc(v.tensor_sub(self.yrows[0:1, 0:CH],
                                          self.yrows[0:1, 0:CH],
                                          self.yrows[0:1, 2 * CH:3 * CH]), "x1v1", 1)
                    self.w(v, "x1sd", c + 1)
                    if c >= 1:
                        self.w(v, "x1st", 32 * c)
                    self.inc(v.reciprocal(self.rstg[0:1, 0:CH],
                                          self.yrows[0:1, 3 * CH:4 * CH]), "vch", 1)
                    self.w(v, "vch")
                    self.inc(v.tensor_tensor(self.yrows[0:1, 5 * CH:6 * CH],
                                             self.yrows[0:1, CH:2 * CH],
                                             self.rstg[0:1, 0:CH],
                                             OP.mult), "vch", 1)
                    self.w(v, "vch")
                    self.inc(v.tensor_scalar(self.nstg[0:1, 0:CH],
                                             self.yrows[0:1, 5 * CH:6 * CH],
                                             -1.0, None, OP.mult), "x1stv", 1)

            @blk.scalar
            def _(sc):
                for c in range(NCH_E):
                    self.w(sc, "yev", (c + 1) * NFT)
                    for ft in range(NFT):
                        ins = sc.activation(self.x2bf[:, CH * ft:CH * (ft + 1)],
                                            self.ybf[:, CH * ft:CH * (ft + 1)],
                                            AF.Square)
                    self.inc(ins, "ysq", 1)
                    self.w(sc, "ys2E", c + 1)
                    self.inc(sc.activation(self.yrows[0:1, 3 * CH:4 * CH],
                                           self.yrows[0:1, 0:CH],
                                           AF.Sqrt, bias=self.epscol[0:1, 0:1]),
                             "ysd", 1)
                    self.w(sc, "x1v", (c + 1) * NFT)
                    self.w(sc, "ys2T", c + 1)
                    for ft in range(NFT):
                        ins = sc.activation(self.x2bf[:, CH * ft:CH * (ft + 1)],
                                            self.xbf[:, CH * ft:CH * (ft + 1)],
                                            AF.Square)
                    self.inc(ins, "x1sq", 1)
                    self.w(sc, "x1v1", c + 1)
                    self.inc(sc.activation(self.yrows[0:1, 3 * CH:4 * CH],
                                           self.yrows[0:1, 0:CH],
                                           AF.Sqrt, bias=self.epscol[0:1, 0:1]),
                             "x1sd", 1)

            @blk.gpsimd
            def _(g):
                for c in range(NCH_E):
                    if c == 0:
                        self.w(g, "sxw")
                    if c >= 1:
                        self.w(g, "mu1v", c)
                    self.inc(g.dma_start(out=self.sxstg[0:1, 0:CH],
                                         in_=self.sxrow_e[c:c + 1, 0:CH]), "sxr", 16)
                    self.w(g, "x1v", (c + 1) * NFT)
                    for ft in range(NFT):
                        self.inc(g.dma_start(
                            out=self.dram["st_x1"][128 * ft:128 * (ft + 1),
                                                   CH * c:CH * (c + 1)],
                            in_=self.x1s[:, CH * ft:CH * (ft + 1)]), "x1w", 16)
                    self.w(g, "x1stv", c + 1)
                    self.inc(g.dma_start(out=self.r1row[c:c + 1, 0:CH],
                                         in_=self.rstg[0:1, 0:CH]), "x1st", 16)
                    self.inc(g.dma_start(out=self.nmr1row[c:c + 1, 0:CH],
                                         in_=self.nstg[0:1, 0:CH]), "x1st", 16)

        with nc.Block() as blk:
            @blk.sync
            def _(sy):
                self.inc(sy.dma_start(out=self.tvrsb[:, :], in_=dram["tvr"]),
                         "tvl", 16)

            @blk.vector
            def _(v):
                self.w(v, "tvl", 16)
                self.w(v, "x1st")
                self.inc(v.tensor_tensor(self.r1row[:, :], self.r1row[:, :],
                                         self.tvrsb[:, :], OP.mult), "vch", 1)
                self.w(v, "vch")
                self.inc(v.tensor_tensor(self.nmr1row[:, :],
                                         self.nmr1row[:, :],
                                         self.tvrsb[:, :], OP.mult), "tvv", 1)
                self.w(v, "tvv", 1)

    def _dbg_dump_merge(self):
        nc = self.nc
        od = self.dout("dbg_oh", (128, EXT), BF16)
        r1d = self.dout("dbg_r1", (NCH_E, CH))
        n1d = self.dout("dbg_nmr1", (NCH_E, CH))
        self.dbg_names += ["dbg_oh", "dbg_r1", "dbg_nmr1"]
        with nc.Block() as blk:
            @blk.gpsimd
            def _(g):
                self.w(g, "ohv")
                self.w(g, "x1st")
                self.inc(g.dma_start(out=od, in_=self.oh_top[:, :]), "dbgst2", 16)
                self.inc(g.dma_start(out=r1d, in_=self.r1row[:, :]), "dbgst2", 16)
                self.inc(g.dma_start(out=n1d, in_=self.nmr1row[:, :]), "dbgst2", 16)
                self.w(g, "dbgst2")


    def declare_inner(self):
        self.st2 = contextlib.ExitStack()
        s2 = lambda n, sh, dt=F32: self.sbuf(n, sh, dt, st=self.st2)
        self.wqkvb = s2("wqkvb", [128, NFT * 1536], BF16)
        self.wob2 = s2("wob2", [128, 4 * DM], BF16)
        self.qhat = s2("qhat", [128, 4 * CH], BF16)
        self.khat = s2("khat", [128, 2 * 4 * CH], BF16)
        self.vtm = s2("vtm", [128, 2 * 4 * 528], BF16)
        self.ohU = s2("ohU", [128, 4 * CH], BF16)
        self.xt2 = s2("xt2", [128, NFT * CH], BF16)
        self.tabs = [s2("tabs0", [128, 4 * CH], BF16)] * 2
        self.w1s = [s2(f"w1s{i}", [128, 2 * NFT * 128], BF16) for i in range(2)]
        self.w2s = [s2(f"w2s{i}", [128, DM], BF16) for i in range(2)]
        self.hbuf = s2("hbuf", [128, 22 * 512], BF16)
        self.pbL = s2("pbL", [128, 4096], BF16)
        self.q2b = s2("q2b", [128, CH], BF16)
        self.t1b = s2("t1b", [128, CH], BF16)
        self.ab = s2("ab", [128, CH], BF16)
        self.bb = s2("bb", [128, CH], BF16)
        self.rqrows = s2("rqrows", [2, CH], BF16)
        self.sd8 = s2("sd8", [128, 8])
        self.rc8 = s2("rc8", [128, 8])
        self.rc8b = s2("rc8b", [128, 8], BF16)
        self.gelub = s2("gelub", [128, 512], BF16)
        self.drcpb = s2("drcpb", [1, 2 * CH], BF16)
        self.halocols = s2("halocols", [128, NFT])
        self.halocolsF = s2("halocolsF", [128, NFT])
        self.bheadc = s2("bheadc", [128, 2], BF16)
        self.bheadb = s2("bheadb", [2, 128], BF16)
        self.psw = s2("psw", [128, 128], BF16)
        self.trim = s2("trim", [128, 256], BF16)
        self.kvc = s2("kvc", [128, 28])
        self.qscol = s2("qscol", [128, 1]); self.qsswcol = s2("qsswcol", [128, 1])
        self.kscol = s2("kscol", [128, 1]); self.ksswcol = s2("ksswcol", [128, 1])
        self.b1acol = s2("b1acol", [128, 22]); self.b1gcol = s2("b1gcol", [128, 22])
        self.b2col = s2("b2col", [128, NFT])
        self.gcol = s2("gcol", [128, NFT]); self.bcol = s2("bcol", [128, NFT])

    def phase_wprep(self, l):
        nc = self.nc
        dram = self.dram
        psA, psB = self.psA, self.psB
        L = str(l)
        P = ""

        with nc.Block() as blk:
            @blk.sync
            def _(sy):
                if l == 0:
                    for nm, t in (("psw", self.psw), ("trimasks", self.trim),
                                  ("kvc", self.kvc)):
                        self.inc(sy.dma_start(out=t[:, :], in_=dram[nm]), P + "cld", 16)

            @blk.gpsimd
            def _(g):
                self.inc(g.dma_start(
                    out=self.wqkvb[:, :],
                    in_=dram["wqkv" + L].rearrange("(a p) c -> p a c", p=128)),
                    P + "wqk", 16)
                self.inc(g.dma_start(
                    out=self.wob2[:, :],
                    in_=dram["wo" + L].rearrange("(a p) c -> p a c", p=128)),
                    P + "wobi", 16)
                self.inc(g.dma_start(out=self.dram["w1bf"], in_=dram["w1" + L]),
                         P + "w1w", 64)
                self.inc(g.dma_start(out=self.dram["w2bf"], in_=dram["w2" + L]),
                         P + "w2w", 16)

        self._wprep_cols(l)

    def _wprep_cols(self, l):
        nc = self.nc
        dram = self.dram
        L = str(l)
        P = ""
        with nc.Block() as blk:
            @blk.sync
            def _(sy):
                self.inc(sy.dma_start(out=self.bheadb[:, :],
                                      in_=dram["bheadb_c"]), P + "crl", 16)
                bc = dram["bcols" + L]
                for dstt, a0, b0 in ((self.b1acol, 0, 22), (self.b1gcol, 22, 44),
                                     (self.b2col, 44, 52), (self.gcol, 52, 60),
                                     (self.bcol, 60, 68)):
                    self.inc(sy.dma_start(out=dstt[:, :], in_=bc[:, a0:b0]),
                             P + "crl", 16)
                qk = dram["qkcols" + L]
                for i, dstt in enumerate((self.qscol, self.qsswcol,
                                          self.kscol, self.ksswcol)):
                    self.inc(sy.dma_start(
                        out=dstt[:, :],
                        in_=qk[i:i + 1, :].rearrange("a (p c) -> p (a c)",
                                                     p=128)), P + "crl", 16)

            @blk.vector
            def _(v):
                self.w(v, P + "crl", 160)
                self.inc(v.memset(self.bheadc[:, :], 0.0), "vch", 1)
                self.w(v, "vch")
                v.memset(self.bheadc[0:64, 0:1], 1.0)
                self.inc(v.memset(self.bheadc[64:128, 1:2], 1.0), "vch", 1)
                self.inc(v.memset(self.vtm.ap()[:, 64:2 * 4 * 528:66], 1.0),
                         P + "colsv", 1)

            @blk.scalar
            def _(sc):
                self.w(sc, P + "crl", 160)

            @blk.tensor
            def _(t):
                self.w(t, P + "crl", 160)


    def phase_inner(self, l):
        nc = self.nc
        dram = self.dram
        psA, psB = self.psA, self.psB
        psBb = psB.ap().bitcast(BF16)
        P = ""
        NCH = NCH_E
        src = dram["st_x1"] if l == 0 else dram["st_in"]
        tabnames = ("csq", "ssq", "csk", "ssk")
        Xbf = self.xbf
        BCH = list(range(0, NCH)) if l == 0 else list(range(1, NCH))
        EFC = list(range(1, NCH)) if l == 0 else list(range(2, NCH))
        GIC = list(range(1, NCH)) if l == 0 else list(range(3, NCH))

        def idx(c):
            bi = BCH.index(c)
            inEF = c in EFC
            ei = EFC.index(c) if inEF else None
            inGI = c in GIC
            zi = GIC.index(c) if inGI else None
            nef = sum(1 for x in EFC if x < c)
            ngi = sum(1 for x in GIC if x < c)
            bp = sum(1 for x in BCH if x < c and x % 2 == c % 2)
            nefp = sum(1 for x in EFC if x < c and x % 2 == c % 2)
            return bi, inEF, ei, inGI, zi, nef, ngi, bp, nefp
        NEG12 = 1e-12

        with nc.Block() as blk:
            @blk.sync
            def _(sy):
                for c in BCH:
                    bi, inEF, ei, inGI, zi, nef, ngi, bp, nefp = idx(c)
                    if bi >= 2:
                        self.w(sy, P + f"inw{c % 2}", 16 * nefp)
                        self.w(sy, P + "xbv", NFT * (bi - 1))
                    self.inc(sy.dma_start(
                        out=self.xs[c % 2].ap().rearrange(
                            "p (a c) -> p a c", a=NFT),
                        in_=src[:, CH * c:CH * (c + 1)].rearrange(
                            "(a p) c -> p a c", p=128)),
                        P + f"ild{c % 2}", 16)
                    if bi >= 1:
                        self.w(sy, P + "qkv2", 8 * bi)
                    for tb in range(4):
                        self.inc(sy.dma_start(
                            out=self.tabs[c % 2][:, CH * tb:CH * (tb + 1)],
                            in_=dram[tabnames[tb]][:, CH * c:CH * (c + 1)]),
                            P + f"tld{c % 2}", 16)
                    if not inGI:
                        continue
                    for j in range(22):
                        g = 22 * zi + j
                        wj = 128 if j < 21 else 42
                        off = 256 * j if j < 21 else 5376
                        if g >= 2:
                            self.w(sy, P + "hT", g - 1)
                        else:
                            self.w(sy, P + "w1w", 64)
                        self.inc(sy.dma_start(
                            out=self.w1s[j % 2][:, 0:16 * wj],
                            in_=dram["w1bf"][:, off:off + 2 * wj].rearrange(
                                "(a p) c -> p a c", p=128)),
                            P + f"w1ld{j % 2}", 16)
                    for kt in range(22):
                        g = 22 * zi + kt
                        r = 128 if kt < 21 else 42
                        if g >= 2:
                            self.w(sy, P + "iT", g - 1)
                        else:
                            self.w(sy, P + "w2w", 16)
                        self.inc(sy.dma_start(
                            out=self.w2s[kt % 2][0:r, :],
                            in_=dram["w2bf"][128 * kt:128 * kt + r, :]),
                            P + f"w2ld{kt % 2}", 16)

            @blk.gpsimd
            def _(g):
                for c in BCH:
                    bi, inEF, ei, inGI, zi, nef, ngi, bp, nefp = idx(c)
                    if l == 0:
                        if bi >= 1:
                            self.w(g, P + "aplT", 2 * bi)
                        self.inc(g.dma_start(out=self.rstg[0:1, 0:CH],
                                             in_=self.r1row[c:c + 1, 0:CH]),
                                 P + "rst", 16)
                        self.inc(g.dma_start(out=self.nstg[0:1, 0:CH],
                                             in_=self.nmr1row[c:c + 1, 0:CH]),
                                 P + "rst", 16)
                    if self.dbg and l == 0 and inEF:
                        self.w(g, P + "xmbf", NFT * (ei + 1))
                        self.inc(g.dma_start(
                            out=self.dram.get("dbgM")[:, 4224 * c:4224 * (c + 1)],
                            in_=Xbf[:, :]), "dbgMw", 16)
                    if inEF:
                        self.w(g, P + "outv", NFT * (ngi + (1 if inGI else 0)))
                        self.w(g, P + "xmidv", NFT * (ei + 1))
                        self.inc(g.dma_start(
                            out=dram["st_in"][:, CH * c:CH * (c + 1)].rearrange(
                                "(a p) c -> p a c", p=128),
                            in_=self.xs[c % 2].ap().rearrange(
                                "p (a c) -> p a c", a=NFT)),
                            P + f"inw{c % 2}", 16)

            @blk.tensor
            def _(t):
                if l == 0:
                    self.w(t, "cld")
                for c in BCH:
                    bi, inEF, ei, inGI, zi, nef, ngi, bp, nefp = idx(c)
                    ov0 = NFT * (ngi - 1) if ngi >= 1 else 0
                    ovn = NFT if ngi >= 1 else 0
                    self.w(t, P + "geS", 22 * ngi)
                    if l == 0:
                        self.w(t, P + "rst", 32 * (bi + 1))
                        if ovn:
                            self.w(t, P + "outv", ov0 + 4)
                        if bi >= 1:
                            self.w(t, P + "xbv", NFT * bi)
                        t.matmul(psA[:, 1024:1536], self.onesrow[0:1, 0:128],
                                 self.rstg[0:1, 0:CH], start=True, stop=True)
                        ins = t.matmul(psA[:, 1536:2048], self.onesrow[0:1, 0:128],
                                       self.nstg[0:1, 0:CH], start=True, stop=True)
                        self.inc(ins, P + "aplT", 2)
                    if bi == 0:
                        self.w(t, P + "wqk", 16)
                    self.w(t, P + "harv", bi + 1)
                    self.w(t, P + "xbv", NFT * (bi + 1))
                    for oc in range(12):
                        gi = 12 * bi + oc
                        if ovn and oc < 2:
                            self.w(t, P + "outv", ov0 + (oc % 2) + 1)
                        if oc == 0:
                            self.w(t, P + "odV", 4 * nef)
                        if gi >= 2:
                            self.w(t, P + "ocv", gi - 1)
                        if 2 <= oc < 10:
                            self.w(t, P + "sqS", 8 * bi + oc - 1)
                        dst = psA[:, 512 * (oc % 2):512 * (oc % 2) + 512]
                        for kt in range(NFT):
                            sh = kt >= 4
                            b = 528 * kt + (15 if sh else 16)
                            ins = t.matmul(dst,
                                           self.wqkvb[:, 1536 * kt + 128 * oc:
                                                      1536 * kt + 128 * (oc + 1)],
                                           Xbf[:, b:b + 512],
                                           start=(kt == 0), stop=(kt == NFT - 1))
                        self.inc(ins, P + "ocmm", 1)
                        if oc < 8:
                            qki = 8 * bi + oc
                            if ovn and oc == 0:
                                self.w(t, P + "outv", ov0 + NFT)
                            self.w(t, P + "sqS", qki + 1)
                            if qki >= 1:
                                self.w(t, P + "ssV", qki)
                            for ts in range(4):
                                ins = t.matmul(psB[:, 8 * oc + 2 * ts:8 * oc + 2 * ts + 2],
                                               self.q2b[:, 128 * ts:128 * (ts + 1)],
                                               self.bheadc[:, 0:2],
                                               start=True, stop=True)
                            self.inc(ins, P + "ssT", 1)
                            self.w(t, P + "ssV2", qki + 1)
                            if qki >= 1:
                                self.w(t, P + "rqE", qki)
                            for ts in range(4):
                                ins = t.transpose(
                                    psBb[0:2, 1024 + 128 * ts:1024 + 128 * (ts + 1)],
                                    self.rc8b[:, 2 * ts:2 * ts + 2],
                                    self.identb[:, :])
                            self.inc(ins, P + "rqT", 1)
                            self.w(t, P + "rqE2", qki + 1)
                            if qki >= 1:
                                self.w(t, P + "t1v", qki)
                            ins = t.matmul(psA[:, 1536:2048], self.bheadb[:, :],
                                           self.rqrows[0:2, 0:CH],
                                           start=True, stop=True)
                            self.inc(ins, P + "plT", 1)
                            self.w(t, P + "t1v", qki + 1)
                            if qki >= 1:
                                self.w(t, P + "bv", qki)
                            ins = t.matmul(psA[:, 1024:1536], self.psw[:, :],
                                           self.t1b[:, :], start=True, stop=True)
                            self.inc(ins, P + "swT", 1)
                        else:
                            vi = 4 * bi + (oc - 8)
                            self.w(t, P + "vbE", vi + 1)
                            if vi >= 1:
                                self.w(t, P + "vtE", vi)
                            for ts in range(4):
                                ins = t.transpose(
                                    psBb[:, 2048 + 128 * ts:2048 + 128 * (ts + 1)],
                                    self.vraw[:, 128 * ts:128 * (ts + 1)],
                                    self.identb[:, :])
                            self.inc(ins, P + "vtT", 1)
                    if not inEF:
                        continue
                    if ovn:
                        self.w(t, P + "outv", ov0 + NFT)
                    self.w(t, P + "qkv2", 8 * (bi + 1))
                    self.w(t, P + "vtE", 4 * (bi + 1))
                    kks = list(range(4 * c - 4, 4 * c + 4))
                    NKK = 8
                    u0c = 32 * ei
                    def emit_plane(hp):
                        self.w(t, P + "drV", 4 * ei + hp + 1)
                        if 4 * ei + hp >= 1:
                            self.w(t, P + "normv", 4 * ei + hp)
                        t.matmul(psB[0:64, 1536:2048], self.onesb[0:1, 0:64],
                                 self.drcpb[0:1, 0:CH], start=True, stop=True)
                        ins = t.matmul(psB[64:128, 1536:2048],
                                       self.onesb[0:1, 0:64],
                                       self.drcpb[0:1, CH:2 * CH],
                                       start=True, stop=True,
                                       tile_position=(0, 64))
                        self.inc(ins, P + "dpT", 1)

                    for hp in range(4):
                        emitted_pv = 0
                        for i, kk in enumerate(kks):
                            u = u0c + hp * NKK + i
                            slot = u % 2
                            if hp >= 1 and i == 2:
                                emit_plane(hp - 1)
                            if u >= 2:
                                self.w(t, P + "exS", u - 1)
                            for h in range(2):
                                lh = self.khat[64 * h:64 * (h + 1),
                                               ((kk // 4) % 2) * 4 * CH + hp * CH +
                                               128 * (kk % 4):
                                               ((kk // 4) % 2) * 4 * CH + hp * CH +
                                               128 * (kk % 4) + 128]
                                ins = t.matmul(
                                    psA[:, 1024 * slot + 512 * h:
                                        1024 * slot + 512 * h + 512],
                                    lh, self.qhat[64 * h:64 * (h + 1),
                                                  hp * CH:hp * CH + 512],
                                    start=True, stop=True,
                                    tile_position=(64 * h, 0))
                            self.inc(ins, P + "sT", 1)
                            if i >= 2:
                                up = u - 2
                                kp = kks[i - 2]
                                self.w(t, P + "mkv", up + 1)
                                if i - 2 == 0:
                                    self.w(t, P + "odV", 4 * ei + hp)
                                for h in range(2):
                                    vb = (((kp // 4) % 2) * 4 * 528 +
                                          528 * (kp % 4) + 132 * hp + 66 * h)
                                    ins = t.matmul(
                                        psB[0:65, 512 * h:512 * h + 512],
                                        self.vtm[:, vb:vb + 65],
                                        self.pbL[:, 1024 * (up % 4) + 512 * h:
                                                 1024 * (up % 4) + 512 * h + 512],
                                        start=(i - 2 == 0), stop=False)
                                self.inc(ins, P + "pvT", 1)
                                emitted_pv += 1
                        for ii in (NKK - 2, NKK - 1):
                            u = u0c + hp * NKK + ii
                            kp = kks[ii]
                            self.w(t, P + "mkv", u + 1)
                            for h in range(2):
                                vb = (((kp // 4) % 2) * 4 * 528 +
                                      528 * (kp % 4) + 132 * hp + 66 * h)
                                ins = t.matmul(
                                    psB[0:65, 512 * h:512 * h + 512],
                                    self.vtm[:, vb:vb + 65],
                                    self.pbL[:, 1024 * (u % 4) + 512 * h:
                                             1024 * (u % 4) + 512 * h + 512],
                                    start=False, stop=(ii == NKK - 1))
                            self.inc(ins, P + "pvT", 1)
                        if hp == 3:
                            emit_plane(hp)
                    if ei == 0:
                        self.w(t, P + "wobi", 16)
                    self.w(t, P + "normv", 4 * (ei + 1))
                    for fc in range(NFT):
                        gi = NFT * ei + fc
                        if gi >= 2:
                            self.w(t, P + "xmidv", gi - 1)
                        dst = psA[:, 512 * (fc % 2):512 * (fc % 2) + 512]
                        for kt in range(4):
                            ins = t.matmul(dst,
                                           self.wob2[:, 1024 * kt + 128 * fc:
                                                     1024 * kt + 128 * (fc + 1)],
                                           self.ohU[:, CH * kt:CH * (kt + 1)],
                                           start=(kt == 0), stop=(kt == 3))
                        self.inc(ins, P + "ymm", 1)
                    if not inGI:
                        continue
                    self.w(t, P + "sqv", zi + 1)
                    self.w(t, P + "exS", 32 * (ei + 1))
                    if zi >= 1:
                        self.w(t, P + "lsV", zi)
                    for ft in range(NFT):
                        b = 528 * ft + (16 if ft < 4 else 15)
                        ins = t.matmul(psA[0:1, 1024:1536], self.onesb[:, 0:1],
                                       Xbf[:, b:b + 512],
                                       start=(ft == 0), stop=(ft == NFT - 1))
                    self.inc(ins, P + "lsT", 1)
                    for ft in range(NFT):
                        b = 528 * ft + (16 if ft < 4 else 15)
                        ins = t.matmul(psA[0:1, 1536:2048], self.onesb[:, 0:1],
                                       self.x2bf[:, b:b + 512],
                                       start=(ft == 0), stop=(ft == NFT - 1))
                    self.inc(ins, P + "lsT", 1)
                    self.w(t, P + "lsV2", zi + 1)
                    self.w(t, P + "xmidv", NFT * (ei + 1))
                    t.matmul(psA[:, 0:512], self.onesrow[0:1, 0:128],
                             self.yrows[0:1, 4 * CH:5 * CH], start=True, stop=True)
                    ins = t.matmul(psA[:, 512:1024], self.onesrow[0:1, 0:128],
                                   self.yrows[0:1, CH:2 * CH], start=True, stop=True)
                    self.inc(ins, P + "l2pT", 1)
                    self.w(t, P + "xt2v", NFT * (zi + 1))
                    for j in range(22):
                        gj = 22 * zi + j
                        wj = 128 if j < 21 else 42
                        self.w(t, P + f"w1ld{j % 2}",
                               16 * (11 * zi + j // 2 + 1))
                        if gj >= 2:
                            self.w(t, P + "hv", gj - 1)
                        bA = 1024 * (j % 2)
                        for kt in range(NFT):
                            t.matmul(psA[0:wj, bA:bA + 512],
                                     self.w1s[j % 2][:, 2 * wj * kt:
                                                    2 * wj * kt + wj],
                                     self.xt2[:, CH * kt:CH * kt + 512],
                                     start=(kt == 0), stop=(kt == NFT - 1))
                        for kt in range(NFT):
                            ins = t.matmul(psA[0:wj, bA + 512:bA + 1024],
                                           self.w1s[j % 2][:, 2 * wj * kt + wj:
                                                          2 * wj * kt + 2 * wj],
                                           self.xt2[:, CH * kt:CH * kt + 512],
                                           start=(kt == 0), stop=(kt == NFT - 1))
                        self.inc(ins, P + "hT", 1)
                    self.w(t, P + "hv", 22 * (zi + 1))
                    for kt in range(22):
                        r = 128 if kt < 21 else 42
                        self.w(t, P + f"w2ld{kt % 2}",
                               16 * (11 * zi + kt // 2 + 1))
                        for fc in range(NFT):
                            ps = psA if fc < 4 else psB
                            dst = ps[:, 512 * (fc % 4):512 * (fc % 4) + 512]
                            ins = t.matmul(dst,
                                           self.w2s[kt % 2][0:r, 128 * fc:
                                                           128 * (fc + 1)],
                                           self.hbuf[0:r, 512 * kt:512 * (kt + 1)],
                                           start=(kt == 0), stop=(kt == 21))
                        self.inc(ins, P + "iT", 1)

            @blk.scalar
            def _(sc):
                for c in BCH:
                    bi, inEF, ei, inGI, zi, nef, ngi, bp, nefp = idx(c)
                    self.w(sc, P + f"ild{c % 2}", 16 * (bp + 1))
                    self.w(sc, P + "sqv", ngi)
                    self.w(sc, P + "ocmm", 12 * bi)
                    self.w(sc, P + "lsT", 2 * ngi)
                    self.w(sc, P + "xt2v", NFT * ngi)
                    self.w(sc, P + "hsv", nef)
                    self.w(sc, P + "hsa", bi)
                    for ft in range(NFT):
                        if l == 0:
                            self.w(sc, P + "apv", NFT * bi + ft + 1)
                        ins = sc.activation(
                            Xbf[:, 528 * ft + 16:528 * ft + 528],
                            self.xs[c % 2][:, CH * ft:CH * (ft + 1)], AF.Copy)
                        self.inc(ins, P + "xbv", 1)
                    for oc in range(8):
                        qki = 8 * bi + oc
                        self.w(sc, P + "ocmm", 12 * bi + oc + 1)
                        self.inc(sc.activation(self.q2b[:, :],
                                               psA[:, 512 * (oc % 2):
                                                   512 * (oc % 2) + 512],
                                               AF.Square), P + "sqS", 1)
                        self.w(sc, P + "ssT", qki + 1)
                        self.inc(sc.activation(self.sd8[:, :],
                                               psB[:, 8 * oc:8 * oc + 8],
                                               AF.Sqrt), P + "l2sS", 1)
                    if not inEF:
                        continue
                    kks = list(range(4 * c - 4, 4 * c + 4))
                    NKK = 8
                    u0c = 32 * ei
                    for hp in range(4):
                        for i, kk in enumerate(kks):
                            u = u0c + hp * NKK + i
                            d = kk - 4 * c
                            qlo = 128 * max(0, d)
                            qhi = 128 * (min(3, d + 4) + 1)
                            self.w(sc, P + "sT", u + 1)
                            if u >= 4:
                                self.w(sc, P + "pvT", u - 3)
                            bS = 1024 * (u % 2)
                            bL = 1024 * (u % 4)
                            if qlo == 0 and qhi == 512:
                                ins = sc.activation(self.pbL[:, bL:bL + 1024],
                                                    psA[:, bS:bS + 1024],
                                                    AF.Exp)
                            else:
                                for h in range(2):
                                    ins = sc.activation(
                                        self.pbL[:, bL + 512 * h + qlo:
                                                 bL + 512 * h + qhi],
                                        psA[:, bS + 512 * h + qlo:
                                            bS + 512 * h + qhi],
                                        AF.Exp)
                            self.inc(ins, P + "exS", 1)
                    if self.dbg and l == 0:
                        self.w(sc, P + "normv", 4 * (ei + 1))
                        self.inc(sc.dma_start(
                            out=self.dram.get("dbgOH")[:, 2048 * c:2048 * (c + 1)],
                            in_=self.ohU[:, :]), "dbgOHw", 16)
                    self.w(sc, P + "ocmm", 12 * (bi + 1))
                    self.w(sc, P + "hsa", bi + 1)
                    for fc in range(NFT):
                        self.w(sc, P + "xmidv", NFT * ei + fc + 1)
                        self.inc(sc.activation(
                            Xbf[:, 528 * fc + 16:528 * fc + 528],
                            self.xs[c % 2][:, CH * fc:CH * (fc + 1)], AF.Copy),
                            P + "xmbf", 1)
                    if not inGI:
                        continue
                    self.w(sc, P + "xmbf", NFT * (ei + 1))
                    self.w(sc, P + "hfrv", ei + 1)
                    for ft in range(NFT):
                        b = 528 * ft + 15
                        ins = sc.activation(self.x2bf[:, b:b + 513],
                                            Xbf[:, b:b + 513], AF.Square)
                    self.inc(ins, P + "sqv", 1)
                    self.w(sc, P + "lsV1", zi + 1)
                    self.inc(sc.activation(self.yrows[0:1, 3 * CH:4 * CH],
                                           self.yrows[0:1, 2 * CH:3 * CH],
                                           AF.Sqrt, bias=self.epscol[0:1, 0:1]),
                             P + "lsS", 1)
                    for j in range(22):
                        gj = 22 * zi + j
                        wj = 128 if j < 21 else 42
                        bA = 1024 * (j % 2)
                        self.w(sc, P + "hT", gj + 1)
                        if gj >= 1:
                            self.w(sc, P + "hv", gj)
                        self.inc(sc.activation(self.gelub[0:wj, 0:512],
                                               psA[0:wj, bA + 512:bA + 1024],
                                               AF.Gelu,
                                               bias=self.b1gcol[0:wj, j:j + 1]),
                                 P + "geS", 1)

            @blk.vector
            def _(v):
                for c in BCH:
                    bi, inEF, ei, inGI, zi, nef, ngi, bp, nefp = idx(c)
                    if l == 0 and c == BCH[0]:
                        self.w(v, "cld")
                    self.w(v, P + "tld0" if c % 2 == 0 else P + "tld1", 64 * (bp + 1))
                    v.tensor_scalar(self.tabs[c % 2][:, 0:CH],
                                    self.tabs[c % 2][:, 0:CH],
                                    self.qscol[:, 0:1], 8.0, OP.mult, OP.mult)
                    v.tensor_scalar(self.tabs[c % 2][:, CH:2 * CH],
                                    self.tabs[c % 2][:, CH:2 * CH],
                                    self.qsswcol[:, 0:1], 8.0, OP.mult, OP.mult)
                    v.tensor_scalar(self.tabs[c % 2][:, 2 * CH:3 * CH],
                                    self.tabs[c % 2][:, 2 * CH:3 * CH],
                                    self.kscol[:, 0:1], None, OP.mult)
                    self.inc(v.tensor_scalar(self.tabs[c % 2][:, 3 * CH:4 * CH],
                                             self.tabs[c % 2][:, 3 * CH:4 * CH],
                                             self.ksswcol[:, 0:1], None, OP.mult),
                             P + "tabv", 1)
                    self.w(v, P + f"ild{c % 2}", 16 * (bp + 1))
                    self.w(v, P + "sqv", ngi)
                    self.w(v, P + "ocmm", 12 * bi)
                    if self.dbg and l == 0 and nef >= 1:
                        self.w(v, "dbgMw", 16 * nef)
                    if c == BCH[0]:
                        ins = v.memset(Xbf.ap()[:, 15:528 * NFT:528], 0.0)
                    else:
                        ins = v.tensor_copy(Xbf.ap()[:, 15:528 * NFT:528],
                                            self.halocols[:, 0:NFT])
                    self.inc(ins, P + "harv", 1)
                    self.w(v, P + "harv", bi + 1)
                    if l == 0:
                        self.w(v, P + "aplT", 2 * (bi + 1))
                        for ft in range(NFT):
                            self.inc(v.tensor_tensor(
                                self.plane[:, :],
                                self.xs[c % 2][:, CH * ft:CH * (ft + 1)],
                                psA[:, 1024:1536], OP.mult), "vch", 1)
                            self.w(v, "vch")
                            self.inc(v.tensor_add(
                                self.xs[c % 2][:, CH * ft:CH * (ft + 1)],
                                self.plane[:, :], psA[:, 1536:2048]),
                                P + "apv", 1)
                            self.w(v, P + "apv", NFT * bi + ft + 1)
                    self.w(v, P + "xbv", NFT * (bi + 1))
                    self.inc(v.tensor_copy(self.halocols[:, 0:NFT],
                                           Xbf.ap()[:, 527:528 * NFT:528]),
                             P + "hsa", 1)
                    for oc in range(12):
                        gi = 12 * bi + oc
                        self.w(v, P + "ocmm", gi + 1)
                        if oc < 8:
                            qki = 8 * bi + oc
                            self.w(v, P + "l2sS", qki + 1)
                            self.inc(v.tensor_scalar(self.rc8[:, :], self.sd8[:, :],
                                                     NEG12, None, OP.max), "vch", 1)
                            self.w(v, "vch")
                            self.inc(v.reciprocal(self.rc8[:, :], self.rc8[:, :]),
                                     P + "ssV", 1)
                            self.w(v, P + "ssV")
                            self.inc(v.tensor_copy(self.rc8b[:, :], self.rc8[:, :]),
                                     P + "ssV2", 1)
                            self.w(v, P + "rqT", qki + 1)
                            self.inc(v.tensor_copy(self.rqrows[0:2, 0:CH],
                                                   psBb[0:2, 1024:1536]),
                                     P + "rqE", 1)
                            self.inc(v.engine_nop(), P + "rqE2", 1)
                            self.w(v, P + "plT", qki + 1)
                            self.inc(v.tensor_copy(self.plane[:, :],
                                                   psA[:, 1536:2048]), "vch", 1)
                            self.w(v, "vch")
                            if qki >= 1:
                                self.w(v, P + "swT", qki)
                            self.inc(v.tensor_tensor(
                                self.t1b[:, :],
                                psA[:, 512 * (oc % 2):512 * (oc % 2) + 512],
                                self.plane[:, :], OP.mult), P + "t1v", 1)
                            self.w(v, P + "swT", qki + 1)
                            tb = 1 if oc < 4 else 3
                            self.inc(v.tensor_tensor(
                                self.bb[:, :], psA[:, 1024:1536],
                                self.tabs[c % 2][:, CH * tb:CH * (tb + 1)],
                                OP.mult), P + "bv", 1)
                            self.inc(v.tensor_tensor(
                                self.ab[:, :], self.t1b[:, :],
                                self.tabs[c % 2][:, CH * (0 if oc < 4 else 2):
                                                 CH * (1 if oc < 4 else 3)],
                                OP.mult), "vch", 1)
                            self.w(v, "vch")
                            if oc == 0:
                                self.w(v, P + "sT", 32 * nef)
                            hp = oc % 4
                            if oc < 4:
                                dst = self.qhat[:, hp * CH:(hp + 1) * CH]
                            else:
                                dst = self.khat[:, (c % 2) * 4 * CH + hp * CH:
                                                (c % 2) * 4 * CH + (hp + 1) * CH]
                            self.inc(v.tensor_tensor(dst, self.ab[:, :],
                                                     self.bb[:, :], OP.add),
                                     P + "qkv2", 1)
                            self.inc(v.engine_nop(), P + "ocv", 1)
                        else:
                            hpv = oc - 8
                            vi = 4 * bi + hpv
                            if hpv == 0:
                                self.w(v, P + "pvT", 32 * nef)
                            self.inc(v.tensor_copy(
                                self.vraw[:, :],
                                psA[:, 512 * (oc % 2):512 * (oc % 2) + 512]),
                                P + "vbE", 1)
                            self.w(v, P + "vtT", vi + 1)
                            for ts in range(4):
                                base = (c % 2) * 4 * 528 + 528 * ts + 132 * hpv
                                ins = v.tensor_copy(
                                    self.vtm.ap()[:, base:base + 132]
                                    .rearrange("p (h x) -> p h x", h=2)[:, :, 0:64],
                                    psBb[:, 2048 + 128 * ts:2048 + 128 * (ts + 1)]
                                    .rearrange("p (h x) -> p h x", h=2))
                            self.inc(ins, P + "vtE", 1)
                            self.inc(v.engine_nop(), P + "ocv", 1)
                    if not inEF:
                        continue
                    kks = list(range(4 * c - 4, 4 * c + 4))
                    NKK = 8
                    u0c = 32 * ei
                    for hp in range(4):
                        for i, kk in enumerate(kks):
                            u = u0c + hp * NKK + i
                            d = kk - 4 * c
                            qlo = 128 * max(0, d)
                            qhi = 128 * (min(3, d + 4) + 1)
                            if u >= 4:
                                self.w(v, P + "pvT", u - 3)
                            for h in range(2):
                                base = 1024 * (u % 4) + 512 * h
                                if qlo > 0:
                                    self.inc(v.memset(
                                        self.pbL[:, base:base + qlo], 0.0),
                                        "vch", 1)
                                if qhi < 512:
                                    self.inc(v.memset(
                                        self.pbL[:, base + qhi:base + 512], 0.0),
                                        "vch", 1)
                            self.w(v, P + "exS", u + 1)
                            q = 128 * d if d >= 0 else 128 * (d + 4)
                            m = self.trim[:, 0:128] if d >= 0 \
                                else self.trim[:, 128:256]
                            for h in range(2):
                                base = 1024 * (u % 4) + 512 * h
                                ins = v.tensor_tensor(
                                    self.pbL[:, base + q:base + q + 128],
                                    self.pbL[:, base + q:base + q + 128],
                                    m, OP.mult)
                            if kk < 12:
                                self.inc(ins, "vch", 1)
                                self.w(v, "vch")
                                for h in range(2):
                                    base = 1024 * (u % 4) + 512 * h
                                    ins = v.tensor_scalar(
                                        self.pbL[:, base:base + 512],
                                        self.pbL[:, base:base + 512],
                                        self.kvc[:, kk:kk + 1], None, OP.mult)
                            self.inc(ins, P + "mkv", 1)
                        self.w(v, P + "pvT", u0c + (hp + 1) * NKK)
                        self.w(v, P + "ymm", NFT * ei)
                        if self.dbg and l == 0 and ei >= 1 and hp == 0:
                            self.w(v, "dbgOHw", 16 * ei)
                        v.tensor_copy(self.ohU[0:64, hp * CH:(hp + 1) * CH],
                                      psB[0:64, 0:512])
                        self.inc(v.tensor_copy(
                            self.ohU[64:128, hp * CH:(hp + 1) * CH],
                            psB[0:64, 512:1024]), P + "odV", 1)
                        self.inc(v.tensor_scalar(self.drcpb[0:1, 0:CH],
                                                 psB[64:65, 0:512],
                                                 1e-30, None, OP.max), "vch", 1)
                        self.inc(v.tensor_scalar(self.drcpb[0:1, CH:2 * CH],
                                                 psB[64:65, 512:1024],
                                                 1e-30, None, OP.max),
                                 P + "drV", 1)
                        self.w(v, P + "dpT", 4 * ei + hp + 1)
                        self.inc(v.reciprocal(psB[:, 1024:1536],
                                              psB[:, 1536:2048]), "vch", 1)
                        self.w(v, "vch")
                        self.inc(v.tensor_tensor(
                            self.ohU[:, hp * CH:(hp + 1) * CH],
                            self.ohU[:, hp * CH:(hp + 1) * CH],
                            psB[:, 1024:1536], OP.mult), P + "normv", 1)
                    if c == EFC[0]:
                        ins = v.memset(Xbf.ap()[:, 15:528 * NFT:528], 0.0)
                    else:
                        ins = v.tensor_copy(Xbf.ap()[:, 15:528 * NFT:528],
                                            self.halocolsF[:, 0:NFT])
                    self.inc(ins, P + "hfrv", 1)
                    for fc in range(NFT):
                        self.w(v, P + "ymm", NFT * ei + fc + 1)
                        self.inc(v.tensor_add(
                            self.xs[c % 2][:, CH * fc:CH * (fc + 1)],
                            self.xs[c % 2][:, CH * fc:CH * (fc + 1)],
                            psA[:, 512 * (fc % 2):512 * (fc % 2) + 512]),
                            P + "xmidv", 1)
                        self.w(v, P + "xmidv")
                    self.w(v, P + "xmbf", NFT * (ei + 1))
                    self.inc(v.tensor_copy(self.halocolsF[:, 0:NFT],
                                           Xbf.ap()[:, 527:528 * NFT:528]),
                             P + "hsv", 1)
                    if not inGI:
                        continue
                    self.w(v, P + "lsT", 2 * (zi + 1))
                    self.inc(v.tensor_scalar(self.yrows[0:1, 0:CH],
                                             psA[0:1, 1024:1536], 1.0 / DM,
                                             None, OP.mult), "vch", 1)
                    self.w(v, "vch")
                    self.inc(v.tensor_tensor(self.yrows[0:1, CH:2 * CH],
                                             self.yrows[0:1, 0:CH],
                                             self.yrows[0:1, 0:CH], OP.mult),
                             "vch", 1)
                    self.w(v, "vch")
                    self.inc(v.scalar_tensor_tensor(
                        self.yrows[0:1, 2 * CH:3 * CH], psA[0:1, 1536:2048],
                        1.0 / DM, self.yrows[0:1, CH:2 * CH],
                        OP.mult, OP.subtract), P + "lsV1", 1)
                    self.w(v, P + "lsS", zi + 1)
                    self.inc(v.reciprocal(self.yrows[0:1, 4 * CH:5 * CH],
                                          self.yrows[0:1, 3 * CH:4 * CH]), "vch", 1)
                    self.w(v, "vch")
                    self.inc(v.tensor_tensor(self.yrows[0:1, 5 * CH:6 * CH],
                                             self.yrows[0:1, 0:CH],
                                             self.yrows[0:1, 4 * CH:5 * CH],
                                             OP.mult), "vch", 1)
                    self.w(v, "vch")
                    self.inc(v.tensor_scalar(self.yrows[0:1, CH:2 * CH],
                                             self.yrows[0:1, 5 * CH:6 * CH],
                                             -1.0, None, OP.mult), P + "lsV2", 1)
                    self.inc(v.engine_nop(), P + "lsV", 1)
                    self.w(v, P + "l2pT", zi + 1)
                    for ft in range(NFT):
                        bsh = 528 * ft + (16 if ft < 4 else 15)
                        self.inc(v.scalar_tensor_tensor(
                            self.xt2[:, CH * ft:CH * (ft + 1)],
                            Xbf[:, bsh:bsh + 512],
                            self.gcol[:, ft:ft + 1], psA[:, 0:512],
                            OP.mult, OP.mult), "vch", 1)
                        self.w(v, "vch")
                        self.inc(v.scalar_tensor_tensor(
                            self.xt2[:, CH * ft:CH * (ft + 1)],
                            psA[:, 512:1024], self.gcol[:, ft:ft + 1],
                            self.xt2[:, CH * ft:CH * (ft + 1)],
                            OP.mult, OP.add), "vch", 1)
                        self.w(v, "vch")
                        self.inc(v.tensor_scalar(
                            self.xt2[:, CH * ft:CH * (ft + 1)],
                            self.xt2[:, CH * ft:CH * (ft + 1)],
                            self.bcol[:, ft:ft + 1], None, OP.add),
                            P + "xt2v", 1)
                    for j in range(22):
                        gj = 22 * zi + j
                        wj = 128 if j < 21 else 42
                        bA = 1024 * (j % 2)
                        self.w(v, P + "geS", gj + 1)
                        self.inc(v.scalar_tensor_tensor(
                            self.hbuf[0:wj, 512 * j:512 * (j + 1)],
                            psA[0:wj, bA:bA + 512],
                            self.b1acol[0:wj, j:j + 1],
                            self.gelub[0:wj, 0:512], OP.add, OP.mult),
                            P + "hv", 1)
                    self.w(v, P + "iT", 22 * (zi + 1))
                    for fc in range(NFT):
                        ps = psA if fc < 4 else psB
                        self.inc(v.scalar_tensor_tensor(
                            self.xs[c % 2][:, CH * fc:CH * fc + 512],
                            ps[:, 512 * (fc % 4):512 * (fc % 4) + 512],
                            self.b2col[:, fc:fc + 1],
                            self.xs[c % 2][:, CH * fc:CH * fc + 512],
                            OP.add, OP.add), P + "outv", 1)


    def phase_final(self):
        nc = self.nc
        dram = self.dram
        psA = self.psA
        P = "F_"
        FCS = list(range(3, NCH_E))
        st3 = contextlib.ExitStack()
        xf1 = self.sbuf("xf1", [128, 2 * NFT * CH], st=st3)
        yt = self.sbuf("yt", [128, CH], st=st3)
        tt = self.sbuf("tt", [128, CH], st=st3)
        with st3:
            with nc.Block() as blk:
                @blk.sync
                def _(sy):
                    for fi, c in enumerate(FCS):
                        if fi >= 2:
                            self.w(sy, P + "yv", NFT * (fi - 1))
                        self.inc(sy.dma_start(
                            out=self.xs[c % 2].ap().rearrange(
                                "p (a c) -> p a c", a=NFT),
                            in_=dram["st_in"][:, CH * c:CH * (c + 1)].rearrange(
                                "(a p) c -> p a c", p=128)),
                            P + f"ild{c % 2}", 16)
                        if fi >= 2:
                            self.w(sy, P + "yv2", NFT * (fi - 1))
                        off = (c % 2) * NFT * CH
                        self.inc(sy.dma_start(
                            out=xf1.ap()[:, off:off + NFT * CH].rearrange(
                                "p (a c) -> p a c", a=NFT),
                            in_=dram["st_x1"][:, CH * c:CH * (c + 1)].rearrange(
                                "(a p) c -> p a c", p=128)),
                            P + f"xld{c % 2}", 16)

                @blk.vector
                def _(v):
                    for fi, c in enumerate(FCS):
                        self.w(v, P + f"ild{c % 2}", 16 * (fi // 2 + 1))
                        self.w(v, P + "sq", fi)
                        for ft in range(NFT):
                            ins = v.tensor_copy(self.xbf[:, CH * ft:CH * (ft + 1)],
                                                self.xs[c % 2][:, CH * ft:CH * (ft + 1)])
                        self.inc(ins, P + "cv", 1)
                        self.w(v, P + "sT", 2 * (fi + 1))
                        self.inc(v.tensor_scalar(self.yrows[0:1, 0:CH],
                                                 psA[0:1, 1024:1536], 1.0 / DM,
                                                 None, OP.mult), "vch", 1)
                        self.w(v, "vch")
                        self.inc(v.tensor_tensor(self.yrows[0:1, CH:2 * CH],
                                                 self.yrows[0:1, 0:CH],
                                                 self.yrows[0:1, 0:CH], OP.mult),
                                 "vch", 1)
                        self.w(v, "vch")
                        self.inc(v.scalar_tensor_tensor(
                            self.yrows[0:1, 2 * CH:3 * CH], psA[0:1, 1536:2048],
                            1.0 / DM, self.yrows[0:1, CH:2 * CH],
                            OP.mult, OP.subtract), P + "v1", 1)
                        self.w(v, P + "sd", fi + 1)
                        self.inc(v.reciprocal(self.yrows[0:1, 4 * CH:5 * CH],
                                              self.yrows[0:1, 3 * CH:4 * CH]),
                                 "vch", 1)
                        self.w(v, "vch")
                        self.inc(v.tensor_tensor(self.yrows[0:1, 5 * CH:6 * CH],
                                                 self.yrows[0:1, 0:CH],
                                                 self.yrows[0:1, 4 * CH:5 * CH],
                                                 OP.mult), "vch", 1)
                        self.w(v, "vch")
                        self.inc(v.tensor_scalar(self.yrows[0:1, CH:2 * CH],
                                                 self.yrows[0:1, 5 * CH:6 * CH],
                                                 -1.0, None, OP.mult), P + "v2", 1)
                        self.w(v, P + "pT", fi + 1)
                        self.w(v, P + f"xld{c % 2}", 16 * (fi // 2 + 1))
                        for ft in range(NFT):
                            if fi >= 1 or ft >= 1:
                                self.w(v, P + "yv3", NFT * fi + ft)
                            self.inc(v.tensor_tensor(
                                tt[:, :], self.xbf[:, CH * ft:CH * (ft + 1)],
                                psA[:, 0:512], OP.mult), "vch", 1)
                            self.w(v, "vch")
                            self.inc(v.tensor_add(tt[:, :], tt[:, :],
                                                  psA[:, 512:1024]), "vch", 1)
                            self.w(v, "vch")
                            if fi >= 1 or ft >= 1:
                                self.w(v, P + "yw", 16 * (NFT * fi + ft - 1) + 16)
                            off = (c % 2) * NFT * CH
                            self.inc(v.tensor_add(yt[:, :], tt[:, :],
                                                  xf1[:, off + CH * ft:off + CH * (ft + 1)]),
                                     P + "yv3", 1)
                            self.inc(v.engine_nop(), P + "yv", 1)
                            self.inc(v.engine_nop(), P + "yv2", 1)

                @blk.gpsimd
                def _(g):
                    for fi, c in enumerate(FCS):
                        for ft in range(NFT):
                            self.w(g, P + "yv3", NFT * fi + ft + 1)
                            self.inc(g.dma_start(
                                out=dram["y"][128 * ft:128 * (ft + 1),
                                              CH * c:CH * (c + 1)],
                                in_=yt[:, :]), P + "yw", 16)
                        if fi == len(FCS) - 1:
                            self.w(g, P + "yw", 16 * NFT * len(FCS))

                @blk.tensor
                def _(t):
                    for fi, c in enumerate(FCS):
                        self.w(t, P + "sq", fi + 1)
                        if fi >= 1:
                            self.w(t, P + "v1", fi)
                        for ft in range(NFT):
                            ins = t.matmul(psA[0:1, 1024:1536], self.onesb[:, 0:1],
                                           self.xbf[:, CH * ft:CH * (ft + 1)],
                                           start=(ft == 0), stop=(ft == NFT - 1))
                        self.inc(ins, P + "sT", 1)
                        for ft in range(NFT):
                            ins = t.matmul(psA[0:1, 1536:2048], self.onesb[:, 0:1],
                                           self.x2bf[:, CH * ft:CH * (ft + 1)],
                                           start=(ft == 0), stop=(ft == NFT - 1))
                        self.inc(ins, P + "sT", 1)
                        self.w(t, P + "v2", fi + 1)
                        if fi >= 1:
                            self.w(t, P + "yv3", NFT * fi)
                        t.matmul(psA[:, 0:512], self.onesrow[0:1, 0:128],
                                 self.yrows[0:1, 4 * CH:5 * CH],
                                 start=True, stop=True)
                        ins = t.matmul(psA[:, 512:1024], self.onesrow[0:1, 0:128],
                                       self.yrows[0:1, CH:2 * CH],
                                       start=True, stop=True)
                        self.inc(ins, P + "pT", 1)

                @blk.scalar
                def _(sc):
                    for fi, c in enumerate(FCS):
                        self.w(sc, P + "cv", fi + 1)
                        for ft in range(NFT):
                            b = CH * ft
                            ins = sc.activation(self.x2bf[:, b:b + 512],
                                                self.xbf[:, b:b + 512],
                                                AF.Square)
                        self.inc(ins, P + "sq", 1)
                        self.w(sc, P + "v1", fi + 1)
                        self.inc(sc.activation(self.yrows[0:1, 3 * CH:4 * CH],
                                               self.yrows[0:1, 2 * CH:3 * CH],
                                               AF.Sqrt, bias=self.epscol[0:1, 0:1]),
                                 P + "sd", 1)

    def finish(self):
        nc = self.nc
        with nc.Block() as blk:
            @blk.vector
            def _(v):
                self.inc(v.memset(self.plane[:, :], 0.0), "fin_v", 1)

            @blk.gpsimd
            def _(g):
                self.w(g, "fin_v", 1)
                for fc in range(NFT):
                    for i in range(NCH_E):
                        self.inc(g.dma_start(
                            out=self.dram["y"][128 * fc:128 * (fc + 1), CH * i:CH * (i + 1)],
                            in_=self.plane[:, :]), "youtst", 16)
                self.w(g, "youtst")

    def build(self):
        self.declare()
        with self.st:
            self.phase_init()
            if self.phases >= 1:
                self.phase_top_kvq()
            if self.phases >= 2:
                self.phase_top_attn()
            if self.phases >= 3:
                self.phase_top_merge()
            if self.dbg and self.phases == 3:
                self._dbg_dump_merge()
            self.st1.close()
            if self.phases >= 4:
                self.phase_boundary()
                self.declare_inner()
                with self.st2:
                    for l in range(min(2, self.phases - 3)):
                        self.phase_wprep(l)
                        self.phase_inner(l)
                        self.phase_boundary()
            if self.phases >= 6:
                self.phase_final()
            else:
                self.finish()
        return self.nc


def make_tables(e0):
    t = np.arange(e0, e0 + EXT, dtype=np.float64)
    half = np.arange(0, DH, 2, dtype=np.float64)
    inv_freq = 1.0 / 10000.0 ** (half / DH)
    freqs = np.concatenate([t[:, None] * inv_freq[None, :]] * 2, 1)
    base = (half + 0.4 * DH) / (1.4 * DH)
    power = (t - SEQ // 2) / float(WIN // 2)
    scale = np.concatenate([base[None, :] ** power[:, None]] * 2, 1)
    cos, sin = np.cos(freqs), np.sin(freqs)
    sgn = np.where(np.arange(DH) < 32, -1.0, 1.0)[None, :]
    mk = lambda a: np.ascontiguousarray(np.tile(a.T, (2, 1))).astype(BF)
    return (mk(cos * scale), mk(sin * scale * sgn),
            mk(cos / scale), mk(sin / scale * sgn))


def make_in_maps(inputs):
    x = np.asarray(inputs["L0_attn_x"], np.float32)
    maps = []
    shared = {
        "wq": np.asarray(inputs["L0_attn_w_q"], np.float32),
        "wk": np.asarray(inputs["L0_attn_w_k"], np.float32),
        "wv": np.asarray(inputs["L0_attn_w_v"], np.float32),
        "wo_top": np.asarray(inputs["L0_attn_w_o"], np.float32),
        "mask_top": np.ascontiguousarray(
            np.asarray(inputs["L0_attn_head_dim_mask"], np.int32).reshape(DTOP, 1)),
        "identb": np.eye(128, dtype=BF),
        "identf": np.eye(128, dtype=np.float32),
        "psw": np.eye(128, dtype=BF)[np.arange(128) ^ 32],
        "trimasks": np.concatenate(
            [(np.arange(128)[:, None] <= np.arange(128)[None, :]),
             (np.arange(128)[:, None] >= np.arange(128)[None, :])],
            1).astype(BF),
        "bheadb_c": np.kron(np.eye(2), np.ones((1, 64))).astype(BF),
    }
    for l in range(2):
        shared[f"wqkv{l}"] = np.ascontiguousarray(np.asarray(inputs["L0_ffn_wqkv"], np.float32)[l])
        shared[f"wo{l}"] = np.ascontiguousarray(np.asarray(inputs["L0_ffn_wo"], np.float32)[l])
        w1l = np.asarray(inputs["L0_ffn_w1"], np.float32)[l]
        shared[f"w1{l}"] = np.ascontiguousarray(np.concatenate(
            [np.concatenate([w1l[:, 128 * j:min(128 * (j + 1), HID)],
                             w1l[:, HID + 128 * j:min(HID + 128 * (j + 1), 2 * HID)]], 1)
             for j in range(22)], 1))
        shared[f"w2{l}"] = np.ascontiguousarray(np.asarray(inputs["L0_ffn_w2"], np.float32)[l])
        shared[f"b1{l}"] = np.ascontiguousarray(np.asarray(inputs["L0_ffn_b1"], np.float32)[l].reshape(1, HID2))
        shared[f"b2{l}"] = np.ascontiguousarray(np.asarray(inputs["L0_ffn_b2"], np.float32)[l].reshape(1, DM))
        shared[f"lng{l}"] = np.ascontiguousarray(np.asarray(inputs["L0_ffn_ln_g"], np.float32)[l].reshape(1, DM))
        shared[f"lnb{l}"] = np.ascontiguousarray(np.asarray(inputs["L0_ffn_ln_b"], np.float32)[l].reshape(1, DM))
        shared[f"qs{l}"] = np.ascontiguousarray(np.asarray(inputs["L0_ffn_q_scale"], np.float32)[l].reshape(1, DH))
        shared[f"ks{l}"] = np.ascontiguousarray(np.asarray(inputs["L0_ffn_k_scale"], np.float32)[l].reshape(1, DH))
        b1v = np.asarray(inputs["L0_ffn_b1"], np.float32)[l]
        b2v = np.asarray(inputs["L0_ffn_b2"], np.float32)[l]
        lgv = np.asarray(inputs["L0_ffn_ln_g"], np.float32)[l]
        lbv = np.asarray(inputs["L0_ffn_ln_b"], np.float32)[l]
        bc = np.zeros((128, 68), np.float32)
        for j in range(22):
            w = 128 if j < 21 else 42
            bc[0:w, j] = b1v[128 * j:128 * j + w]
            bc[0:w, 22 + j] = b1v[HID + 128 * j:HID + 128 * j + w]
        for fc in range(NFT):
            bc[:, 44 + fc] = b2v[128 * fc:128 * (fc + 1)]
            bc[:, 52 + fc] = lgv[128 * fc:128 * (fc + 1)]
            bc[:, 60 + fc] = lbv[128 * fc:128 * (fc + 1)]
        shared[f"bcols{l}"] = np.ascontiguousarray(bc)
        qk = np.zeros((4, 128), np.float32)
        qsv = np.tile(np.asarray(inputs["L0_ffn_q_scale"], np.float32)[l], 2)
        ksv = np.tile(np.asarray(inputs["L0_ffn_k_scale"], np.float32)[l], 2)
        sw = lambda a: np.concatenate([a[32:64], a[0:32], a[96:128], a[64:96]])
        qk[0], qk[1], qk[2], qk[3] = qsv, sw(qsv), ksv, sw(ksv)
        shared[f"qkcols{l}"] = np.ascontiguousarray(qk)

    xT = [np.ascontiguousarray(x[b].T) for b in range(NB)]
    e0s = []
    for core in range(8):
        b, q = divmod(core, 4)
        a0 = OWN * q
        e0 = a0 - HALO
        e0s.append(e0)
        csq, ssq, csk, ssk = make_tables(e0)
        m = dict(shared)
        m["xT"] = xT[b]
        if e0 < 0:
            m["xTe"] = np.ascontiguousarray(np.concatenate(
                [np.zeros((DM, -e0), np.float32), xT[b][:, 0:e0 + EXT]], 1))
        else:
            m["xTe"] = np.ascontiguousarray(xT[b][:, e0:e0 + EXT])
        m["csq"], m["ssq"], m["csk"], m["ssk"] = csq, ssq, csk, ssk
        blk0 = np.arange(28) * 128 + e0
        m["kvc"] = np.ascontiguousarray(np.broadcast_to(
            (blk0 >= 0).astype(np.float32)[None, :], (128, 28)))
        tglob = np.arange(EXT) + e0
        m["tvr"] = np.ascontiguousarray(
            (tglob >= 0).astype(np.float32).reshape(NCH_E, CH))
        maps.append(m)
    return maps, e0s


def kernel(**inputs):
    maps, e0s = make_in_maps(inputs)
    b = B(dbg=False)
    nc = b.build()
    res = run_bass_kernel_spmd(nc, maps, core_ids=list(range(8)))
    out = np.empty((NB, SEQ, DM), np.float32)
    for core in range(8):
        bi, q = divmod(core, 4)
        a0 = OWN * q
        yc = res.results[core]["y"]
        out[bi, a0:a0 + OWN, :] = yc[:, HALO:HALO + OWN].T
    return out
